# version 10
# baseline (speedup 1.0000x reference)
import contextlib
import numpy as np
import ml_dtypes

import concourse.bass as bass
import concourse.mybir as mybir
from concourse.bass_utils import run_bass_kernel_spmd

F32 = mybir.dt.float32
BF16 = mybir.dt.bfloat16
AF = mybir.ActivationFunctionType
ALU = mybir.AluOpType
AX = mybir.AxisListType

NT = 16
NHEAD = 4
DK = 256
DV = 512
D = 1024
NE = 8
EPS = 1e-5

PE, DVE, ACT, SP, POOL = "pe", "dve", "act", "sp", "pool"
ENG_SEM = {PE: "s_pe", DVE: "s_dve", ACT: "s_act", POOL: "s_pool"}


class Plan:
    def __init__(self):
        self.stages = []
        self.semcnt = {}
        self.last_w = {}
        self.readers = {}
        self.waited = {e: {} for e in (PE, DVE, ACT, SP, POOL)}

    def stage(self, eng, emit, reads=(), writes=(), dma=None):
        waits = {}
        own = ENG_SEM.get(eng)

        def add(dep):
            if dep is None:
                return
            sem, val = dep
            if sem == own:
                return
            if self.waited[eng].get(sem, -1) >= val:
                return
            if waits.get(sem, -1) < val:
                waits[sem] = val

        for rs in reads:
            add(self.last_w.get(rs))
        for ws in writes:
            add(self.last_w.get(ws))
            for sem, val in self.readers.get(ws, {}).items():
                add((sem, val))
        for s, v in waits.items():
            self.waited[eng][s] = v
        if dma is not None:
            self.semcnt[dma] = self.semcnt.get(dma, 0) + 16
            comp = (dma, self.semcnt[dma])
        else:
            s = ENG_SEM[eng]
            self.semcnt[s] = self.semcnt.get(s, 0) + 1
            comp = (s, self.semcnt[s])
        for rs in reads:
            d = self.readers.setdefault(rs, {})
            if d.get(comp[0], -1) < comp[1]:
                d[comp[0]] = comp[1]
        for ws in writes:
            self.last_w[ws] = comp
            self.readers[ws] = {}
        self.stages.append((eng, sorted(waits.items()), emit, comp))


DEBUG = False


def build_nc():
    nc = bass.Bass("TRN2", target_bir_lowering=False)

    dt = {}
    def din(name, shape, dtype=F32):
        dt[name] = nc.dram_tensor(name, shape, dtype, kind="ExternalInput").ap()
        return dt[name]

    x_h = din("x", [2048, 1024])
    wq_h = din("wq", [128, 8192])
    wk_h = din("wk", [128, 8192])
    wv_h = din("wv", [128, 16384])
    wg_h = din("wg", [128, 16384])
    wo_h = din("wo", [128, 16384])
    wgk_h = din("wgk", [128, 32])
    rw_h = din("rw", [128, 64])
    bgk_h = din("bgk", [1, 4])
    bin_h = din("bin", [1, 8192])
    bout_h = din("bout", [1, 8192])
    gnw4_h = din("gnw4", [128, 4])
    idbf_h = din("idbf", [128, 128], BF16)
    utri_h = din("utri", [128, 128])
    onesn_h = din("onesn", [128, 128])
    maskbf_h = din("maskbf", [128, 128], BF16)
    win_h = [din(f"win{e}", [128, 8192]) for e in range(NE)]
    wout_h = [din(f"wout{e}", [128, 8192]) for e in range(NE)]
    out_h = nc.dram_tensor("out", [2048, 1024], F32, kind="ExternalOutput").ap()

    plan = Plan()

    with contextlib.ExitStack() as ctx:
        E = ctx.enter_context

        W1 = E(nc.sbuf_tensor("W1", [128, 32768], BF16))
        W2 = E(nc.sbuf_tensor("W2", [128, 16384], BF16))
        W3 = E(nc.sbuf_tensor("W3", [128, 16384], BF16))
        xpfm = E(nc.sbuf_tensor("xpfm", [128, 16384], BF16))
        S = E(nc.sbuf_tensor("S", [128, 4096], BF16))
        csb = E(nc.sbuf_tensor("csb", [128, 128], F32))
        xf = E(nc.sbuf_tensor("xf", [128, 2048], F32))
        xb = E(nc.sbuf_tensor("xb", [128, 1024], BF16))
        xfm = E(nc.sbuf_tensor("xfm", [128, 1024], BF16))
        q2 = E(nc.sbuf_tensor("q2", [128, 1024], BF16))
        k1 = E(nc.sbuf_tensor("k1", [128, 1024], BF16))
        qfm = E(nc.sbuf_tensor("qfm", [128, 1024], BF16))
        kfm = E(nc.sbuf_tensor("kfm", [128, 1024], BF16))
        vog = E(nc.sbuf_tensor("vog", [128, 2048], BF16))
        gsil = E(nc.sbuf_tensor("gsil", [128, 2048], BF16))
        on_ = E(nc.sbuf_tensor("on_", [128, 2048], BF16))
        r_ = E(nc.sbuf_tensor("r_", [128, 1024], F32))
        diag = E(nc.sbuf_tensor("diag", [128, 128], BF16))
        At = E(nc.sbuf_tensor("At", [128, 128], BF16))
        idbf = E(nc.sbuf_tensor("idbf_s", [128, 128], BF16))
        utri = E(nc.sbuf_tensor("utri_s", [128, 128], F32))
        onesn = E(nc.sbuf_tensor("onesn_s", [128, 128], F32))
        maskbf = E(nc.sbuf_tensor("maskbf_s", [128, 128], BF16))
        gnw4 = E(nc.sbuf_tensor("gnw4_s", [128, 4], F32))
        wgk = E(nc.sbuf_tensor("wgk_s", [128, 32], BF16))
        rw = E(nc.sbuf_tensor("rw_s", [128, 64], BF16))
        bgkr = E(nc.sbuf_tensor("bgkr_s", [1, 4], BF16))
        onesr = E(nc.sbuf_tensor("onesr", [1, 128], BF16))
        epsv = E(nc.sbuf_tensor("epsv", [128, 1], F32))
        onev = E(nc.sbuf_tensor("onev", [128, 1], F32))
        t4 = E(nc.sbuf_tensor("t4", [128, 4], F32))
        gsp = E(nc.sbuf_tensor("gsp", [128, 4], F32))
        eqv = E(nc.sbuf_tensor("eqv", [128, 4], F32))
        ekv = E(nc.sbuf_tensor("ekv", [128, 4], F32))
        ekdv = E(nc.sbuf_tensor("ekdv", [128, 4], F32))
        egtb = E(nc.sbuf_tensor("egtb", [128, 4], F32))
        stv = E(nc.sbuf_tensor("stv", [128, 12], F32))
        rhv = E(nc.sbuf_tensor("rhv", [128, 4], F32))
        pr = E(nc.sbuf_tensor("pr", [128, 8], F32))
        mbb = E(nc.sbuf_tensor("mbb", [128, 8], F32))
        msk = E(nc.sbuf_tensor("msk", [128, 8], F32))
        p2 = E(nc.sbuf_tensor("p2", [128, 8], F32))
        rta = E(nc.sbuf_tensor("rta", [128, 8], F32))

        PAp = E(nc.psum_tensor("PA", [128, 1024], F32))
        PBp = E(nc.psum_tensor("PB", [128, 1024], F32))
        PCp = E(nc.psum_tensor("PC", [128, 512], F32))
        PDp = E(nc.psum_tensor("PD", [128, 512], F32))
        T0 = E(nc.psum_tensor("T0", [128, 1024], BF16))
        T1 = E(nc.psum_tensor("T1", [128, 1024], BF16))
        Tb = [T0, T1]

        sem_names = ["s_pe", "s_dve", "s_act", "s_pool", "d_c", "d_sc", "d_wqk", "d_wqk2", "d_wv",
                     "d_wg", "d_wo", "d_bin", "d_x0", "d_x1", "d_out0", "d_out1",
                     "d_e0a", "d_e0b", "d_e1a", "d_e1b"]
        sems = {n: E(nc.semaphore(n)) for n in sem_names}

        st = plan.stage

        def mset(v):
            v.memset(S[:, :], 0.0)
            v.memset(epsv[:, :], EPS)
            v.memset(onev[:, :], 1.0)
            return v.memset(onesr[:, :], 1.0)
        st(DVE, mset, writes=["S0", "S1", "S2", "S3", "cstm", "onesr"])

        for name, hbm, sb in [("idbf", idbf_h, idbf), ("utri", utri_h, utri),
                              ("onesn", onesn_h, onesn), ("maskbf", maskbf_h, maskbf),
                              ("gnw4", gnw4_h, gnw4)]:
            st(SP, (lambda e, h=hbm, s=sb: e.dma_start(out=s[:, :], in_=h)),
               writes=[name], dma="d_c")
        for name, hbm, sb in [("wgk", wgk_h, wgk), ("rw", rw_h, rw), ("bgkr", bgk_h, bgkr)]:
            st(POOL, (lambda e, h=hbm, s=sb: e.dma_start(out=s[:, :], in_=h)),
               writes=[name], dma="d_sc")
        st(POOL, lambda e: e.dma_start(out=W1[:, 0:8192], in_=wq_h, max_dma_last_dim=8192),
           writes=["W1a_lo", "W1a_hi"], dma="d_wqk")
        st(POOL, lambda e: e.dma_start(out=W1[:, 8192:16384], in_=wk_h, max_dma_last_dim=8192),
           writes=["W1b_lo", "W1b_hi"], dma="d_wqk2")
        st(POOL, lambda e: e.dma_start(out=W1[:, 16384:32768], in_=wv_h, max_dma_last_dim=8192),
           writes=["W1c", "W1d"], dma="d_wv")
        st(POOL, lambda e: e.dma_start(out=W2[:, :], in_=wg_h, max_dma_last_dim=8192),
           writes=["W2"], dma="d_wg")
        st(POOL, lambda e: e.dma_start(out=W3[:, :], in_=wo_h, max_dma_last_dim=8192),
           writes=["W3"], dma="d_wo")

        for rs in ("idbf", "utri", "onesn", "maskbf", "gnw4"):
            plan.last_w[rs] = ("d_c", plan.semcnt["d_c"])
        for rs in ("wgk", "rw", "bgkr"):
            plan.last_w[rs] = ("d_sc", plan.semcnt["d_sc"])
        for rs in ("W1a_lo", "W1a_hi"):
            plan.last_w[rs] = ("d_wqk", plan.semcnt["d_wqk"])
        for rs in ("W1b_lo", "W1b_hi"):
            plan.last_w[rs] = ("d_wqk2", plan.semcnt["d_wqk2"])

        def gnw_scale(v):
            for cc in range(16):
                i = v.tensor_scalar(W3[:, cc * 1024:(cc + 1) * 1024],
                                    W3[:, cc * 1024:(cc + 1) * 1024],
                                    gnw4[:, cc % 4:cc % 4 + 1], None, ALU.mult, ALU.bypass)
            return i
        st(DVE, gnw_scale, reads=["gnw4"], writes=["W3"])

        if DEBUG:
            dbg_lg = nc.dram_tensor("dbg_lg", [128, 128], F32, kind="ExternalOutput").ap()
            dbg_pr = nc.dram_tensor("dbg_pr", [128, 128], F32, kind="ExternalOutput").ap()
            dbg_g = nc.dram_tensor("dbg_g", [128, 24], F32, kind="ExternalOutput").ap()
            dbg_r = nc.dram_tensor("dbg_r", [128, 1024], F32, kind="ExternalOutput").ap()
            dbg_on0 = nc.dram_tensor("dbg_on0", [128, 2048], BF16, kind="ExternalOutput").ap()
            dbg_xb = nc.dram_tensor("dbg_xb", [128, 1024], BF16, kind="ExternalOutput").ap()
            dbg_stv = nc.dram_tensor("dbg_stv", [128, 12], F32, kind="ExternalOutput").ap()
            dbg_rt = nc.dram_tensor("dbg_rt", [128, 32], F32, kind="ExternalOutput").ap()

        def load_x(t):
            xs = t % 2
            st(SP, (lambda e, t=t, xs=xs: e.dma_start(
                out=xf[:, xs * 1024:(xs + 1) * 1024], in_=x_h[t * 128:(t + 1) * 128, :])),
               writes=[f"xf{xs}"], dma=f"d_x{xs}")

        def cast_x(t):
            xs = t % 2
            st(ACT, (lambda e, xs=xs: e.activation(xb[:, :], xf[:, xs * 1024:(xs + 1) * 1024], AF.Copy)),
               reads=[f"xf{xs}"], writes=["xb"])

        load_x(0)
        cast_x(0)
        load_x(1)

        def emit_xtr():
            def xtr(e):
                for c in range(8):
                    i = e.transpose(T0[:, c * 128:(c + 1) * 128], xb[:, c * 128:(c + 1) * 128], idbf[:, :])
                return i
            st(PE, xtr, reads=["xb", "idbf"], writes=["T0"])
            st(ACT, lambda e: e.activation(xfm[:, :], T0[:, :], AF.Copy),
               reads=["T0"], writes=["xfm"])

        def emit_gkfront():
            def gk_mm(e):
                for c in range(8):
                    e.matmul(PCp[:, 0:4], xfm[:, c * 128:(c + 1) * 128], wgk[:, c * 4:(c + 1) * 4],
                             start=(c == 0), stop=False)
                return e.matmul(PCp[:, 0:4], onesr[:, :], bgkr[:, :], start=False, stop=True)
            st(PE, gk_mm, reads=["xfm", "wgk", "onesr", "bgkr", "cstm"], writes=["PC"])

            def g_chain(e):
                e.activation(t4[:, :], PCp[:, 0:4], AF.Exp, scale=-1.0)
                return e.activation(gsp[:, :], t4[:, :], AF.Ln, bias=onev[:, 0:1])
            st(ACT, g_chain, reads=["PC", "cstm"], writes=["gv"])

            def gc_mm(e):
                e.matmul(PCp[:, 4:8], utri[:, :], gsp[:, :], start=True, stop=True)
                return e.matmul(PCp[:, 8:12], onesn[:, :], gsp[:, :], start=True, stop=True)
            st(PE, gc_mm, reads=["utri", "onesn", "gv"], writes=["PC"])

            def exps(e):
                e.activation(eqv[:, :], PCp[:, 4:8], AF.Exp)
                e.activation(ekv[:, :], PCp[:, 4:8], AF.Exp, scale=-1.0)
                return e.activation(egtb[:, :], PCp[:, 8:12], AF.Exp)
            st(ACT, exps, reads=["PC"], writes=["gv"])

            st(DVE, lambda e: e.tensor_tensor(ekdv[:, :], ekv[:, :], egtb[:, :], ALU.mult),
               reads=["gv"], writes=["gv"])
        def emit_seg2(t):
                def xptr(e):
                    for c in range(8):
                        i = e.transpose(T0[:, c * 128:(c + 1) * 128], gsil[:, c * 128:(c + 1) * 128], idbf[:, :])
                    return i
                st(PE, xptr, reads=["gsil", "idbf"], writes=["T0"])
                st(ACT, (lambda e, t=t: e.activation(xpfm[:, t * 1024:(t + 1) * 1024], T0[:, :], AF.Copy)),
                   reads=["T0"], writes=["xpfm"])

                def router_mm(e, t=t):
                    for c in range(8):
                        i = e.matmul(PCp[:, 0:8], xpfm[:, t * 1024 + c * 128:t * 1024 + (c + 1) * 128],
                                     rw[:, c * 8:(c + 1) * 8], start=(c == 0), stop=(c == 7))
                    return i
                st(PE, router_mm, reads=["xpfm", "rw"], writes=["PC"])

                if DEBUG:
                    st(ACT, (lambda e: e.activation(mbb[:, :], PCp[:, 0:8], AF.Copy)),
                       reads=["PC"], writes=["rt"])
                    st(SP, (lambda e, t=t: e.dma_start(out=dbg_lg[:, t * 8:(t + 1) * 8], in_=mbb[:, :])),
                       reads=["rt"], dma="d_out0")

                st(ACT, lambda e: e.activation(pr[:, :], PCp[:, 0:8], AF.Copy),
                   reads=["PC"], writes=["rt"])

                def rtA(e):
                    sp_ = rta[:, 7:8]
                    e.tensor_scalar(mbb[:, :], pr[:, :], -1.0, None, ALU.mult, ALU.bypass)
                    e.memset(sp_, 0.0)
                    e.memset(sp_, 0.0)
                    e.tensor_tensor(msk[:, 0:4], mbb[:, 0:4], mbb[:, 4:8], ALU.min)
                    e.tensor_tensor(p2[:, 0:4], mbb[:, 0:4], mbb[:, 4:8], ALU.max)
                    e.tensor_tensor(msk[:, 4:6], msk[:, 0:2], msk[:, 2:4], ALU.min)
                    e.tensor_tensor(p2[:, 4:6], msk[:, 0:2], msk[:, 2:4], ALU.max)
                    e.memset(sp_, 0.0)
                    e.tensor_tensor(p2[:, 6:8], p2[:, 0:2], p2[:, 2:4], ALU.min)
                    e.memset(sp_, 0.0)
                    e.tensor_tensor(mbb[:, 0:2], p2[:, 4:6], p2[:, 6:8], ALU.min)
                    e.memset(sp_, 0.0)
                    e.tensor_tensor(msk[:, 1:2], msk[:, 4:5], msk[:, 5:6], ALU.max)
                    e.memset(sp_, 0.0)
                    e.tensor_tensor(mbb[:, 2:3], mbb[:, 0:1], mbb[:, 1:2], ALU.min)
                    e.memset(sp_, 0.0)
                    return e.tensor_tensor(rta[:, 2:3], msk[:, 1:2], mbb[:, 2:3], ALU.min)
                st(DVE, rtA, reads=["rt"], writes=["rt"])

                def rsg(e):
                    e.sign(p2[:, :], pr[:, :], bias=rta[:, 2:3])
                    return e.activation(pr[:, :], pr[:, :], AF.Exp)
                st(ACT, rsg, reads=["rt"], writes=["rt"])

                def rtB(e, t=t):
                    sp_ = rta[:, 7:8]
                    e.tensor_scalar(msk[:, :], p2[:, :], 1.0, 1.0, ALU.add, ALU.min)
                    e.tensor_tensor(mbb[:, 0:4], pr[:, 0:4], pr[:, 4:8], ALU.add)
                    e.memset(sp_, 0.0)
                    e.tensor_tensor(mbb[:, 4:6], mbb[:, 0:2], mbb[:, 2:4], ALU.add)
                    e.memset(sp_, 0.0)
                    e.tensor_tensor(rta[:, 3:4], mbb[:, 4:5], mbb[:, 5:6], ALU.add)
                    e.tensor_tensor(msk[:, :], msk[:, :], pr[:, :], ALU.mult)
                    e.reciprocal(rta[:, 4:5], rta[:, 3:4])
                    e.memset(sp_, 0.0)
                    e.memset(sp_, 0.0)
                    return e.tensor_scalar(csb[:, t * 8:(t + 1) * 8], msk[:, :], rta[:, 4:5], None,
                                           ALU.mult, ALU.bypass)
                st(DVE, rtB, reads=["rt"], writes=["rt", "c"])
                if DEBUG:
                    st(SP, (lambda e, t=t: e.dma_start(out=dbg_pr[:, t * 8:(t + 1) * 8], in_=pr[:, :])),
                       reads=["rt"], dma="d_out0")
                    if t == 5:
                        for i, sb in enumerate([p2, msk, mbb, rta]):
                            st(SP, (lambda e, i=i, sb=sb: e.dma_start(out=dbg_rt[:, i * 8:(i + 1) * 8], in_=sb[:, :])),
                               reads=["rt"], dma="d_out0")

        for t in range(NT):
            xslot = t % 2

            if t == 0:
                emit_xtr()
                emit_gkfront()

            if DEBUG and t == 0:
                for i, sb in enumerate([t4, gsp, eqv, ekv, ekdv, egtb]):
                    st(SP, (lambda e, i=i, sb=sb: e.dma_start(out=dbg_g[:, i * 4:(i + 1) * 4], in_=sb[:, :])),
                       reads=["gv"], dma="d_out0")
                st(SP, lambda e: e.dma_start(out=dbg_xb, in_=xb[:, :]), reads=["xb"], dma="d_out0")

            def q_mm(e):
                for c in range(8):
                    xc = xfm[:, c * 128:(c + 1) * 128]
                    e.matmul(PAp[:, 0:512], xc, W1[:, c * 1024:c * 1024 + 512], start=(c == 0), stop=(c == 7))
                    i = e.matmul(PAp[:, 512:1024], xc, W1[:, c * 1024 + 512:(c + 1) * 1024], start=(c == 0), stop=(c == 7))
                return i
            st(PE, q_mm, reads=["xfm", "W1a_lo", "W1a_hi"], writes=["PA0", "PA1"])

            def k_mm(e):
                for c in range(8):
                    xc = xfm[:, c * 128:(c + 1) * 128]
                    e.matmul(PBp[:, 0:512], xc, W1[:, 8192 + c * 1024:8192 + c * 1024 + 512], start=(c == 0), stop=(c == 7))
                    i = e.matmul(PBp[:, 512:1024], xc, W1[:, 8192 + c * 1024 + 512:8192 + (c + 1) * 1024], start=(c == 0), stop=(c == 7))
                return i
            st(PE, k_mm, reads=["xfm", "W1b_lo", "W1b_hi"], writes=["PB0", "PB1"])

            def q2_ev(e):
                for h in range(NHEAD):
                    i = e.tensor_scalar(q2[:, h * 256:(h + 1) * 256], PAp[:, h * 256:(h + 1) * 256],
                                        eqv[:, h:h + 1], 0.0625, ALU.mult, ALU.mult)
                return i
            st(DVE, q2_ev, reads=["PA0", "PA1", "gv"], writes=["q2"])

            def k1_ev(e):
                for h in range(NHEAD):
                    i = e.tensor_scalar(k1[:, h * 256:(h + 1) * 256], PBp[:, h * 256:(h + 1) * 256],
                                        ekv[:, h:h + 1], None, ALU.mult, ALU.bypass)
                return i
            st(DVE, k1_ev, reads=["PB0", "PB1", "gv"], writes=["k1"])

            def qtr(e):
                for c in range(8):
                    i = e.transpose(T1[:, c * 128:(c + 1) * 128], q2[:, c * 128:(c + 1) * 128], idbf[:, :])
                return i
            st(PE, qtr, reads=["q2", "idbf"], writes=["T1"])
            st(ACT, lambda e: e.activation(qfm[:, :], T1[:, :], AF.Copy), reads=["T1"], writes=["qfm"])

            def ktr(e):
                for c in range(8):
                    i = e.transpose(T0[:, c * 128:(c + 1) * 128], k1[:, c * 128:(c + 1) * 128], idbf[:, :])
                return i
            st(PE, ktr, reads=["k1", "idbf"], writes=["T0"])
            st(ACT, lambda e: e.activation(kfm[:, :], T0[:, :], AF.Copy), reads=["T0"], writes=["kfm"])

            def kd_ev(e):
                for h in range(NHEAD):
                    i = e.tensor_scalar(k1[:, h * 256:(h + 1) * 256], PBp[:, h * 256:(h + 1) * 256],
                                        ekdv[:, h:h + 1], None, ALU.mult, ALU.bypass)
                return i
            st(DVE, kd_ev, reads=["PB0", "PB1", "gv"], writes=["k1"])

            if t > 0:
                emit_seg2(t - 1)

            def v_mm(e, nt0, pb):
                for c in range(8):
                    xc = xfm[:, c * 128:(c + 1) * 128]
                    e.matmul(pb[:, 0:512], xc, W1[:, 16384 + c * 2048 + nt0 * 512:16384 + c * 2048 + (nt0 + 1) * 512],
                             start=(c == 0), stop=(c == 7))
                    i = e.matmul(pb[:, 512:1024], xc, W1[:, 16384 + c * 2048 + (nt0 + 1) * 512:16384 + c * 2048 + (nt0 + 2) * 512],
                                 start=(c == 0), stop=(c == 7))
                return i
            st(PE, (lambda e: v_mm(e, 0, PAp)), reads=["xfm", "W1c", "W1d"], writes=["PA0", "PA1"])
            st(DVE, lambda e: e.tensor_copy(vog[:, 0:1024], PAp[:, :]), reads=["PA0", "PA1"], writes=["vog"])
            st(PE, (lambda e: v_mm(e, 2, PBp)), reads=["xfm", "W1c", "W1d"], writes=["PB0", "PB1"])
            st(DVE, lambda e: e.tensor_copy(vog[:, 1024:2048], PBp[:, :]), reads=["PB0", "PB1"], writes=["vog"])

            def g_mm(e, nt0, pb):
                for c in range(8):
                    xc = xfm[:, c * 128:(c + 1) * 128]
                    e.matmul(pb[:, 0:512], xc, W2[:, c * 2048 + nt0 * 512:c * 2048 + (nt0 + 1) * 512],
                             start=(c == 0), stop=(c == 7))
                    i = e.matmul(pb[:, 512:1024], xc, W2[:, c * 2048 + (nt0 + 1) * 512:c * 2048 + (nt0 + 2) * 512],
                                 start=(c == 0), stop=(c == 7))
                return i
            st(PE, (lambda e: g_mm(e, 0, PAp)), reads=["xfm", "W2"], writes=["PA0", "PA1"])
            st(ACT, lambda e: e.activation(gsil[:, 0:1024], PAp[:, :], AF.Silu),
               reads=["PA0", "PA1"], writes=["gsil"])
            st(PE, (lambda e: g_mm(e, 2, PBp)), reads=["xfm", "W2"], writes=["PB0", "PB1"])
            st(ACT, lambda e: e.activation(gsil[:, 1024:2048], PBp[:, :], AF.Silu),
               reads=["PB0", "PB1"], writes=["gsil"])

            for h in range(NHEAD):
                hp = 0

                def at_mm(e, h=h):
                    e.matmul(PCp[:, 0:128], kfm[:, (2 * h) * 128:(2 * h + 1) * 128],
                             qfm[:, (2 * h) * 128:(2 * h + 1) * 128], start=True, stop=False)
                    return e.matmul(PCp[:, 0:128], kfm[:, (2 * h + 1) * 128:(2 * h + 2) * 128],
                                    qfm[:, (2 * h + 1) * 128:(2 * h + 2) * 128], start=False, stop=True)
                st(PE, at_mm, reads=["kfm", "qfm"], writes=["PC"])

                st(DVE, (lambda e, hp=hp: e.scalar_tensor_tensor(
                    At[:, hp * 128:(hp + 1) * 128], PCp[:, 0:128], 1.0, maskbf[:, :], ALU.mult, ALU.mult)),
                   reads=["PC", "maskbf"], writes=[f"At{hp}"])

                OPp = PDp[:, :] if h % 2 == 0 else PBp[:, 0:512]
                ores = "PD" if h % 2 == 0 else "PB0"

                def o_mm(e, h=h, hp=hp, OPp=OPp):
                    e.matmul(OPp, At[:, hp * 128:(hp + 1) * 128], vog[:, h * 512:(h + 1) * 512],
                             start=True, stop=False)
                    e.matmul(OPp, qfm[:, (2 * h) * 128:(2 * h + 1) * 128],
                             S[:, (2 * h) * 512:(2 * h + 1) * 512], start=False, stop=False)
                    return e.matmul(OPp, qfm[:, (2 * h + 1) * 128:(2 * h + 2) * 128],
                                    S[:, (2 * h + 1) * 512:(2 * h + 2) * 512], start=False, stop=True)
                st(PE, o_mm, reads=[f"At{hp}", "vog", "qfm", f"S{h}"], writes=[ores])

                def s_mm(e, h=h):
                    for cc in range(2):
                        bank = PAp[:, 0:512] if cc == 0 else PAp[:, 512:1024]
                        i = e.matmul(bank, k1[:, h * 256 + cc * 128:h * 256 + (cc + 1) * 128],
                                     vog[:, h * 512:(h + 1) * 512], start=True, stop=True)
                    return i
                st(PE, s_mm, reads=["k1", "vog"], writes=["PA0", "PA1"])

                def rms(e, h=h, OPp=OPp):
                    e.activation(q2[:, 0:512], OPp, AF.Square, accum_out=rhv[:, 0:1])
                    e.activation(rhv[:, 1:2], rhv[:, 0:1], AF.Ln, bias=epsv[:, 0:1], scale=1.0 / 512)
                    return e.activation(rhv[:, 2:3], rhv[:, 1:2], AF.Exp, scale=-0.5)
                st(ACT, rms, reads=[ores, "cstm"], writes=["rh", "q2"])

                st(DVE, (lambda e, h=h, OPp=OPp: e.tensor_scalar(
                    on_[:, h * 512:(h + 1) * 512], OPp, rhv[:, 2:3], None, ALU.mult, ALU.bypass)),
                   reads=[ores, "rh"], writes=["on_"])

                def s_ev(e, h=h):
                    e.scalar_tensor_tensor(S[:, (2 * h) * 512:(2 * h + 1) * 512],
                                           S[:, (2 * h) * 512:(2 * h + 1) * 512], egtb[:, h:h + 1],
                                           PAp[:, 0:512], ALU.mult, ALU.add)
                    return e.scalar_tensor_tensor(S[:, (2 * h + 1) * 512:(2 * h + 2) * 512],
                                                  S[:, (2 * h + 1) * 512:(2 * h + 2) * 512], egtb[:, h:h + 1],
                                                  PAp[:, 512:1024], ALU.mult, ALU.add)
                st(DVE, s_ev, reads=["PA0", "PA1", "gv"], writes=[f"S{h}"])

            st(DVE, lambda e: e.tensor_tensor(gsil[:, :], gsil[:, :], on_[:, :], ALU.mult),
               reads=["gsil", "on_"], writes=["gsil"])

            def ogtr(e, half):
                tb = Tb[half]
                for c in range(8):
                    i = e.transpose(tb[:, c * 128:(c + 1) * 128],
                                    gsil[:, half * 1024 + c * 128:half * 1024 + (c + 1) * 128], idbf[:, :])
                return i
            st(PE, (lambda e: ogtr(e, 0)), reads=["gsil", "idbf"], writes=["T0"])
            st(ACT, lambda e: e.activation(vog[:, 0:1024], T0[:, :], AF.Copy), reads=["T0"], writes=["vog"])
            st(PE, (lambda e: ogtr(e, 1)), reads=["gsil", "idbf"], writes=["T1"])
            st(ACT, lambda e: e.activation(vog[:, 1024:2048], T1[:, :], AF.Copy), reads=["T1"], writes=["vog"])

            if t + 1 < NT:
                cast_x(t + 1)
                emit_xtr()
                emit_gkfront()

            def wo_mm(e):
                for cc in range(16):
                    oc = vog[:, cc * 128:(cc + 1) * 128]
                    e.matmul(PAp[:, 0:512], oc, W3[:, cc * 1024:cc * 1024 + 512], start=(cc == 0), stop=(cc == 15))
                    i = e.matmul(PAp[:, 512:1024], oc, W3[:, cc * 1024 + 512:(cc + 1) * 1024], start=(cc == 0), stop=(cc == 15))
                return i
            st(PE, wo_mm, reads=["vog", "W3"], writes=["PA0", "PA1"])

            def resid(e, xs=xslot):
                e.scalar_tensor_tensor(r_[:, 0:512], PAp[:, 0:512], 1.0, xf[:, xs * 1024:xs * 1024 + 512],
                                       ALU.mult, ALU.add)
                return e.scalar_tensor_tensor(r_[:, 512:1024], PAp[:, 512:1024], 1.0,
                                              xf[:, xs * 1024 + 512:(xs + 1) * 1024],
                                              ALU.mult, ALU.add)
            st(DVE, resid, reads=["PA0", "PA1", f"xf{xslot}"], writes=["r_"])
            if t + 2 < NT:
                load_x(t + 2)

            st(ACT, lambda e: e.activation(q2[:, :], r_[:, :], AF.Identity, accum_out=stv[:, 2:3]),
               reads=["r_"], writes=["stt", "q2"])
            st(DVE, lambda e: e.tensor_scalar(stv[:, 3:4], stv[:, 2:3], -1.0 / 1024, None,
                                              ALU.mult, ALU.bypass),
               reads=["stt"], writes=["stt"])

            def sqc(e):
                e.activation(q2[:, :], r_[:, :], AF.Square, bias=stv[:, 3:4], accum_out=stv[:, 8:9])
                e.activation(stv[:, 9:10], stv[:, 8:9], AF.Ln, bias=epsv[:, 0:1], scale=1.0 / 1024)
                return e.activation(stv[:, 10:11], stv[:, 9:10], AF.Exp, scale=-0.5)
            st(ACT, sqc, reads=["r_", "stt", "cstm"], writes=["stt", "q2"])

            st(DVE, lambda e: e.tensor_scalar(gsil[:, 0:1024], r_[:, :], stv[:, 3:4], stv[:, 10:11],
                                              ALU.add, ALU.mult),
               reads=["r_", "stt"], writes=["gsil"])

            if DEBUG and t == 0:
                st(SP, lambda e: e.dma_start(out=dbg_on0, in_=on_[:, :]), reads=["on_"], dma="d_out0")
                st(SP, lambda e: e.dma_start(out=dbg_r, in_=r_[:, :]), reads=["r_"], dma="d_out0")
                st(SP, lambda e: e.dma_start(out=dbg_stv, in_=stv[:, :]), reads=["stt"], dma="d_out0")

        emit_seg2(NT - 1)

        st(POOL, lambda e: e.dma_start(out=W2[0:1, 0:8192], in_=bin_h, max_dma_last_dim=8192),
           writes=["W2"], dma="d_bin")
        st(POOL, lambda e: e.dma_start(out=W2[0:1, 8192:16384], in_=bout_h, max_dma_last_dim=8192),
           writes=["W2"], dma="d_bin")

        SLOT = [(0, ["W1a_lo", "W1a_hi", "W1b_lo", "W1b_hi"]), (16384, ["W1c", "W1d"])]

        def load_expert(ee):
            base, res = SLOT[ee % 2]
            half = len(res) // 2
            st(POOL, (lambda e, ee=ee, base=base: e.dma_start(
                out=W1[:, base:base + 8192], in_=win_h[ee], max_dma_last_dim=8192)),
               writes=res[:half], dma=f"d_e{ee % 2}a")
            st(POOL, (lambda e, ee=ee, base=base: e.dma_start(
                out=W1[:, base + 8192:base + 16384], in_=wout_h[ee], max_dma_last_dim=8192)),
               writes=res[half:], dma=f"d_e{ee % 2}b")

        load_expert(0)
        load_expert(1)

        PAIRS = [(e_i, t) for e_i in range(NE) for t in range(NT)]

        def b_hmm(i):
            e_i, t = PAIRS[i]
            base, slot_res = SLOT[e_i % 2]
            HPp = PAp if i % 2 == 0 else PBp
            hres = ["PA0", "PA1"] if i % 2 == 0 else ["PB0", "PB1"]
            htm = q2 if i % 2 == 0 else xfm
            hmr = "q2" if i % 2 == 0 else "xfm"

            def h_mm(e, base=base, t=t, e_i=e_i, HPp=HPp):
                for c in range(8):
                    xc = xpfm[:, t * 1024 + c * 128:t * 1024 + (c + 1) * 128]
                    e.matmul(HPp[:, 0:512], xc, W1[:, base + c * 1024:base + c * 1024 + 512],
                             start=(c == 0), stop=False)
                    e.matmul(HPp[:, 512:1024], xc, W1[:, base + c * 1024 + 512:base + (c + 1) * 1024],
                             start=(c == 0), stop=False)
                e.matmul(HPp[:, 0:512], onesr[:, :], W2[0:1, e_i * 1024:e_i * 1024 + 512],
                         start=False, stop=True)
                return e.matmul(HPp[:, 512:1024], onesr[:, :], W2[0:1, e_i * 1024 + 512:(e_i + 1) * 1024],
                                start=False, stop=True)
            st(PE, h_mm, reads=["xpfm"] + slot_res[:len(slot_res) // 2] + ["W2", "onesr", "cstm"], writes=hres)

            def gelu(e, HPp=HPp, htm=htm):
                e.activation(htm[:, 0:512], HPp[:, 0:512], AF.Gelu_apprx_tanh)
                return e.activation(htm[:, 512:1024], HPp[:, 512:1024], AF.Gelu_apprx_tanh)
            st(ACT, gelu, reads=hres, writes=[hmr])

        b_hmm(0)
        b_hmm(1)
        for i, (e_i, t) in enumerate(PAIRS):
            base, slot_res = SLOT[e_i % 2]
            tp = i % 2

            htm = q2 if i % 2 == 0 else xfm
            hmr = "q2" if i % 2 == 0 else "xfm"

            def htr(e, tp=tp, htm=htm):
                tb = Tb[tp]
                for c in range(8):
                    i2 = e.transpose(tb[:, c * 128:(c + 1) * 128], htm[:, c * 128:(c + 1) * 128], idbf[:, :])
                return i2
            st(PE, htr, reads=[hmr, "idbf"], writes=[f"T{tp}"])
            st(ACT, (lambda e, tp=tp: e.activation(k1[:, :], Tb[tp][:, :], AF.Copy)),
               reads=[f"T{tp}"], writes=["k1"])

            if i + 2 < len(PAIRS):
                b_hmm(i + 2)

            if e_i == 0:
                st(DVE, (lambda e, t=t: e.memset(W3[:, t * 1024:(t + 1) * 1024], 0.0)),
                   writes=["W3"])

            def y_mm(e, base=base, e_i=e_i):
                for c in range(8):
                    hc = k1[:, c * 128:(c + 1) * 128]
                    e.matmul(PCp[:, :], hc, W1[:, base + 8192 + c * 1024:base + 8192 + c * 1024 + 512],
                             start=(c == 0), stop=False)
                    e.matmul(PDp[:, :], hc, W1[:, base + 8192 + c * 1024 + 512:base + 8192 + (c + 1) * 1024],
                             start=(c == 0), stop=False)
                e.matmul(PCp[:, :], onesr[:, :], W2[0:1, 8192 + e_i * 1024:8192 + e_i * 1024 + 512],
                         start=False, stop=True)
                return e.matmul(PDp[:, :], onesr[:, :], W2[0:1, 8192 + e_i * 1024 + 512:8192 + (e_i + 1) * 1024],
                                start=False, stop=True)
            st(PE, y_mm, reads=["k1"] + slot_res[len(slot_res) // 2:] + ["W2", "onesr", "cstm"], writes=["PC", "PD"])

            def y_acc(e, t=t, e_i=e_i):
                e.scalar_tensor_tensor(W3[:, t * 1024:t * 1024 + 512], PCp[:, :],
                                       csb[:, t * 8 + e_i:t * 8 + e_i + 1],
                                       W3[:, t * 1024:t * 1024 + 512], ALU.mult, ALU.add)
                return e.scalar_tensor_tensor(W3[:, t * 1024 + 512:(t + 1) * 1024], PDp[:, :],
                                              csb[:, t * 8 + e_i:t * 8 + e_i + 1],
                                              W3[:, t * 1024 + 512:(t + 1) * 1024], ALU.mult, ALU.add)
            st(DVE, y_acc, reads=["PC", "PD", "c", "W3"], writes=["W3"])

            if t == NT - 1 and e_i + 2 < NE:
                load_expert(e_i + 2)

        def c_tr(t):
            tp = t % 2
            xtm = q2 if tp == 0 else k1
            xres = "q2" if tp == 0 else "k1"

            def ctr(e, t=t, tp=tp):
                for c in range(8):
                    i = e.transpose(Tb[tp][:, c * 128:(c + 1) * 128],
                                    xpfm[:, t * 1024 + c * 128:t * 1024 + (c + 1) * 128], idbf[:, :])
                return i
            st(PE, ctr, reads=["xpfm", "idbf"], writes=[f"T{tp}"])
            st(ACT, (lambda e, tp=tp, xtm=xtm: e.activation(xtm[:, :], Tb[tp][:, :], AF.Copy)),
               reads=[f"T{tp}"], writes=[xres])

        c_tr(0)
        c_tr(1)
        for t in range(NT):
            tp = t % 2
            xtm = q2 if tp == 0 else k1
            xres = "q2" if tp == 0 else "k1"

            def cr(e, t=t, xtm=xtm):
                return e.scalar_tensor_tensor(r_[:, :], W3[:, t * 1024:(t + 1) * 1024], 1.0, xtm[:, :],
                                              ALU.mult, ALU.add, accum_out=stv[:, 2:3])
            st(DVE, cr, reads=["W3", xres], writes=["r_", "stt"])

            if t + 2 < NT:
                c_tr(t + 2)

            def csqc(e):
                e.activation(stv[:, 3:4], stv[:, 2:3], AF.Copy, scale=-1.0 / 1024)
                e.activation(kfm[:, :], r_[:, :], AF.Square, bias=stv[:, 3:4], accum_out=stv[:, 8:9])
                e.activation(stv[:, 9:10], stv[:, 8:9], AF.Ln, bias=epsv[:, 0:1], scale=1.0 / 1024)
                return e.activation(stv[:, 10:11], stv[:, 9:10], AF.Exp, scale=-0.5)
            st(ACT, csqc, reads=["r_", "stt", "cstm"], writes=["stt", "kfm"])

            xs = t % 2
            st(DVE, (lambda e, xs=xs: e.tensor_scalar(xf[:, xs * 1024:(xs + 1) * 1024], r_[:, :],
                                                      stv[:, 3:4], stv[:, 10:11], ALU.add, ALU.mult)),
               reads=["r_", "stt"], writes=[f"xf{xs}"])
            st(SP, (lambda e, t=t, xs=xs: e.dma_start(
                out=out_h[t * 128:(t + 1) * 128, :], in_=xf[:, xs * 1024:(xs + 1) * 1024])),
               reads=[f"xf{xs}"], dma=f"d_out{xs}")

        if DEBUG:
            dxp = nc.dram_tensor("dbg_xp", [128, 16384], BF16, kind="ExternalOutput").ap()
            dc = nc.dram_tensor("dbg_c", [128, 128], F32, kind="ExternalOutput").ap()
            dy = nc.dram_tensor("dbg_y", [128, 16384], BF16, kind="ExternalOutput").ap()
            don = nc.dram_tensor("dbg_on", [128, 2048], BF16, kind="ExternalOutput").ap()
            st(SP, lambda e: e.dma_start(out=dxp, in_=xpfm[:, :]), reads=["xpfm"], dma="d_out0")
            st(SP, lambda e: e.dma_start(out=dc, in_=csb[:, :]), reads=["c"], dma="d_out0")
            st(SP, lambda e: e.dma_start(out=dy, in_=W3[:, :]), reads=["W3"], dma="d_out0")
            st(SP, lambda e: e.dma_start(out=don, in_=on_[:, :]), reads=["on_"], dma="d_out0")

        block = E(nc.Block())
        eng_map = {}

        def runner(engname):
            def run(eng):
                for (en, waits, emit, comp) in plan.stages:
                    if en != engname:
                        continue
                    for semname, val in waits:
                        eng.wait_ge(sems[semname], val)
                    inst = emit(eng)
                    inst.then_inc(sems[comp[0]], 16 if comp[0].startswith("d_") else 1)
                if engname == SP:
                    eng.wait_ge(sems["d_out0"], plan.semcnt.get("d_out0", 0))
                    eng.wait_ge(sems["d_out1"], plan.semcnt.get("d_out1", 0))
            return run

        block.tensor(runner(PE))
        block.vector(runner(DVE))
        block.scalar(runner(ACT))
        block.sync(runner(SP))
        block.gpsimd(runner(POOL))

    return nc


def _lay8(w):
    n = w.shape[1]
    return np.ascontiguousarray(w.reshape(8, 128, n).transpose(1, 0, 2).reshape(128, 8 * n))


def _lay16(w):
    n = w.shape[1]
    return np.ascontiguousarray(w.reshape(16, 128, n).transpose(1, 0, 2).reshape(128, 16 * n))


def kernel(L0_attn_x, L0_attn_Wq, L0_attn_Wk, L0_attn_Wv, L0_attn_Wg, L0_attn_Wgk,
           L0_attn_bgk, L0_attn_Wo, L0_attn_gnorm_w, L0_ffn_router_w, L0_ffn_w_in,
           L0_ffn_b_in, L0_ffn_w_out, L0_ffn_b_out):
    x = np.asarray(L0_attn_x, dtype=np.float32)
    shared = {
        "wq": _lay8(np.asarray(L0_attn_Wq, np.float32)),
        "wk": _lay8(np.asarray(L0_attn_Wk, np.float32)),
        "wv": _lay8(np.asarray(L0_attn_Wv, np.float32)),
        "wg": _lay8(np.asarray(L0_attn_Wg, np.float32)),
        "wo": _lay16(np.asarray(L0_attn_Wo, np.float32)),
        "wgk": _lay8(np.asarray(L0_attn_Wgk, np.float32)),
        "rw": _lay8(np.asarray(L0_ffn_router_w, np.float32)),
        "bgk": np.ascontiguousarray(np.asarray(L0_attn_bgk, np.float32).reshape(1, 4)),
        "bin": np.ascontiguousarray(np.asarray(L0_ffn_b_in, np.float32).reshape(1, 8192)),
        "bout": np.ascontiguousarray(np.asarray(L0_ffn_b_out, np.float32).reshape(1, 8192)),
        "gnw4": np.ascontiguousarray(np.asarray(L0_attn_gnorm_w, np.float32).reshape(4, 128).T),
        "idbf": np.eye(128).astype(ml_dtypes.bfloat16),
        "utri": np.ascontiguousarray(np.triu(np.ones((128, 128), np.float32)) * (-1.0 / 16.0)),
        "onesn": np.full((128, 128), -1.0 / 16.0, np.float32),
        "maskbf": np.triu(np.ones((128, 128))).astype(ml_dtypes.bfloat16),
    }
    w_in = np.asarray(L0_ffn_w_in, np.float32)
    w_out = np.asarray(L0_ffn_w_out, np.float32)
    for e in range(NE):
        shared[f"win{e}"] = _lay8(w_in[e])
        shared[f"wout{e}"] = _lay8(w_out[e])

    nc = build_nc()
    in_maps = []
    for c in range(8):
        m = dict(shared)
        m["x"] = np.ascontiguousarray(x[c])
        in_maps.append(m)
    res = run_bass_kernel_spmd(nc, in_maps, core_ids=list(range(8)))
    out = np.stack([np.asarray(res.results[c]["out"]) for c in range(8)], axis=0)
    return out.astype(np.float32)


# version 14
# speedup vs baseline: 1.0460x; 1.0460x over previous
import contextlib
import numpy as np
import ml_dtypes

import concourse.bass as bass
import concourse.mybir as mybir
from concourse.bass_utils import run_bass_kernel_spmd

F32 = mybir.dt.float32
BF16 = mybir.dt.bfloat16
AF = mybir.ActivationFunctionType
ALU = mybir.AluOpType
AX = mybir.AxisListType

NT = 16
NHEAD = 4
DK = 256
DV = 512
D = 1024
NE = 8
EPS = 1e-5

PE, DVE, ACT, SP, POOL = "pe", "dve", "act", "sp", "pool"
ENG_SEM = {PE: "s_pe", DVE: "s_dve", ACT: "s_act", POOL: "s_pool"}


class Plan:
    def __init__(self):
        self.stages = []
        self.semcnt = {}
        self.last_w = {}
        self.readers = {}
        self.waited = {e: {} for e in (PE, DVE, ACT, SP, POOL)}

    def stage(self, eng, emit, reads=(), writes=(), dma=None):
        waits = {}
        own = ENG_SEM.get(eng)

        def add(dep):
            if dep is None:
                return
            sem, val = dep
            if sem == own:
                return
            if self.waited[eng].get(sem, -1) >= val:
                return
            if waits.get(sem, -1) < val:
                waits[sem] = val

        for rs in reads:
            add(self.last_w.get(rs))
        for ws in writes:
            add(self.last_w.get(ws))
            for sem, val in self.readers.get(ws, {}).items():
                add((sem, val))
        for s, v in waits.items():
            self.waited[eng][s] = v
        if dma is not None:
            self.semcnt[dma] = self.semcnt.get(dma, 0) + 16
            comp = (dma, self.semcnt[dma])
        else:
            s = ENG_SEM[eng]
            self.semcnt[s] = self.semcnt.get(s, 0) + 1
            comp = (s, self.semcnt[s])
        for rs in reads:
            d = self.readers.setdefault(rs, {})
            if d.get(comp[0], -1) < comp[1]:
                d[comp[0]] = comp[1]
        for ws in writes:
            self.last_w[ws] = comp
            self.readers[ws] = {}
        self.stages.append((eng, sorted(waits.items()), emit, comp))


DEBUG = False


def build_nc():
    nc = bass.Bass("TRN2", target_bir_lowering=False)

    dt = {}
    def din(name, shape, dtype=F32):
        dt[name] = nc.dram_tensor(name, shape, dtype, kind="ExternalInput").ap()
        return dt[name]

    x_h = din("x", [2048, 1024])
    wq_h = din("wq", [128, 8192])
    wk_h = din("wk", [128, 8192])
    wv_h = din("wv", [128, 16384])
    wg_h = din("wg", [128, 16384])
    wo_h = din("wo", [128, 16384])
    wgk_h = din("wgk", [128, 32])
    rw_h = din("rw", [128, 64])
    bgk_h = din("bgk", [1, 4])
    bin_h = din("bin", [1, 8192])
    bout_h = din("bout", [1, 8192])
    gnw4_h = din("gnw4", [128, 4])
    idbf_h = din("idbf", [128, 128], BF16)
    utri_h = din("utri", [128, 128])
    onesn_h = din("onesn", [128, 128])
    maskbf_h = din("maskbf", [128, 128], BF16)
    win_h = [din(f"win{e}", [128, 8192]) for e in range(NE)]
    wout_h = [din(f"wout{e}", [128, 8192]) for e in range(NE)]
    out_h = nc.dram_tensor("out", [2048, 1024], F32, kind="ExternalOutput").ap()

    plan = Plan()

    with contextlib.ExitStack() as ctx:
        E = ctx.enter_context

        W1 = E(nc.sbuf_tensor("W1", [128, 32768], BF16))
        W2 = E(nc.sbuf_tensor("W2", [128, 16384], BF16))
        W3 = E(nc.sbuf_tensor("W3", [128, 16384], BF16))
        xpfm = E(nc.sbuf_tensor("xpfm", [128, 16384], BF16))
        S = E(nc.sbuf_tensor("S", [128, 4096], BF16))
        csb = E(nc.sbuf_tensor("csb", [128, 128], F32))
        xf = E(nc.sbuf_tensor("xf", [128, 2048], F32))
        xb = E(nc.sbuf_tensor("xb", [128, 1024], BF16))
        xfm = E(nc.sbuf_tensor("xfm", [128, 1024], BF16))
        q2 = E(nc.sbuf_tensor("q2", [128, 1024], BF16))
        k1 = E(nc.sbuf_tensor("k1", [128, 1024], BF16))
        qfm = E(nc.sbuf_tensor("qfm", [128, 1024], BF16))
        kfm = E(nc.sbuf_tensor("kfm", [128, 1024], BF16))
        vog = E(nc.sbuf_tensor("vog", [128, 2048], BF16))
        gsil = E(nc.sbuf_tensor("gsil", [128, 2048], BF16))
        on_ = E(nc.sbuf_tensor("on_", [128, 2048], BF16))
        r_ = E(nc.sbuf_tensor("r_", [128, 1024], F32))
        diag = E(nc.sbuf_tensor("diag", [128, 128], BF16))
        At = E(nc.sbuf_tensor("At", [128, 128], BF16))
        idbf = E(nc.sbuf_tensor("idbf_s", [128, 128], BF16))
        utri = E(nc.sbuf_tensor("utri_s", [128, 128], F32))
        onesn = E(nc.sbuf_tensor("onesn_s", [128, 128], F32))
        maskbf = E(nc.sbuf_tensor("maskbf_s", [128, 128], BF16))
        gnw4 = E(nc.sbuf_tensor("gnw4_s", [128, 4], F32))
        wgk = E(nc.sbuf_tensor("wgk_s", [128, 32], BF16))
        rw = E(nc.sbuf_tensor("rw_s", [128, 64], BF16))
        bgkr = E(nc.sbuf_tensor("bgkr_s", [1, 4], BF16))
        onesr = E(nc.sbuf_tensor("onesr", [1, 128], BF16))
        epsv = E(nc.sbuf_tensor("epsv", [128, 1], F32))
        onev = E(nc.sbuf_tensor("onev", [128, 1], F32))
        t4 = E(nc.sbuf_tensor("t4", [128, 4], F32))
        gsp = E(nc.sbuf_tensor("gsp", [128, 4], F32))
        eqv = E(nc.sbuf_tensor("eqv", [128, 4], F32))
        ekv = E(nc.sbuf_tensor("ekv", [128, 4], F32))
        ekdv = E(nc.sbuf_tensor("ekdv", [128, 4], F32))
        egtb = E(nc.sbuf_tensor("egtb", [128, 4], F32))
        stv = E(nc.sbuf_tensor("stv", [128, 12], F32))
        rhv = E(nc.sbuf_tensor("rhv", [128, 4], F32))
        pr = E(nc.sbuf_tensor("pr", [128, 8], F32))
        mbb = E(nc.sbuf_tensor("mbb", [128, 8], F32))
        msk = E(nc.sbuf_tensor("msk", [128, 8], F32))
        p2 = E(nc.sbuf_tensor("p2", [128, 8], F32))
        rta = E(nc.sbuf_tensor("rta", [128, 8], F32))

        PAp = E(nc.psum_tensor("PA", [128, 1024], F32))
        PBp = E(nc.psum_tensor("PB", [128, 1024], F32))
        PCp = E(nc.psum_tensor("PC", [128, 512], F32))
        PDp = E(nc.psum_tensor("PD", [128, 512], F32))
        T0 = E(nc.psum_tensor("T0", [128, 1024], BF16))
        T1 = E(nc.psum_tensor("T1", [128, 1024], BF16))
        Tb = [T0, T1]

        sem_names = ["s_pe", "s_dve", "s_act", "s_pool", "d_c", "d_sc", "d_wqk", "d_wqk2", "d_wv",
                     "d_wg", "d_wo", "d_bin", "d_x0", "d_x1", "d_out0", "d_out1",
                     "d_e0a", "d_e0b", "d_e1a", "d_e1b"]
        sems = {n: E(nc.semaphore(n)) for n in sem_names}

        st = plan.stage

        def mset(v):
            v.memset(S[:, :], 0.0)
            v.memset(epsv[:, :], EPS)
            v.memset(onev[:, :], 1.0)
            return v.memset(onesr[:, :], 1.0)
        st(DVE, mset, writes=["S0", "S1", "S2", "S3", "cstm", "onesr"])

        for name, hbm, sb in [("idbf", idbf_h, idbf), ("utri", utri_h, utri),
                              ("onesn", onesn_h, onesn), ("maskbf", maskbf_h, maskbf),
                              ("gnw4", gnw4_h, gnw4)]:
            st(SP, (lambda e, h=hbm, s=sb: e.dma_start(out=s[:, :], in_=h)),
               writes=[name], dma="d_c")
        for name, hbm, sb in [("wgk", wgk_h, wgk), ("rw", rw_h, rw), ("bgkr", bgk_h, bgkr)]:
            st(POOL, (lambda e, h=hbm, s=sb: e.dma_start(out=s[:, :], in_=h)),
               writes=[name], dma="d_sc")
        st(POOL, lambda e: e.dma_start(out=W1[:, 0:8192], in_=wq_h, max_dma_last_dim=8192),
           writes=["W1a_lo", "W1a_hi"], dma="d_wqk")
        st(POOL, lambda e: e.dma_start(out=W1[:, 8192:16384], in_=wk_h, max_dma_last_dim=8192),
           writes=["W1b_lo", "W1b_hi"], dma="d_wqk2")
        st(POOL, lambda e: e.dma_start(out=W1[:, 16384:32768], in_=wv_h, max_dma_last_dim=8192),
           writes=["W1c", "W1d"], dma="d_wv")
        st(POOL, lambda e: e.dma_start(out=W2[:, :], in_=wg_h, max_dma_last_dim=8192),
           writes=["W2"], dma="d_wg")
        st(POOL, lambda e: e.dma_start(out=W3[:, :], in_=wo_h, max_dma_last_dim=8192),
           writes=["W3"], dma="d_wo")

        for rs in ("idbf", "utri", "onesn", "maskbf", "gnw4"):
            plan.last_w[rs] = ("d_c", plan.semcnt["d_c"])
        for rs in ("wgk", "rw", "bgkr"):
            plan.last_w[rs] = ("d_sc", plan.semcnt["d_sc"])
        for rs in ("W1a_lo", "W1a_hi"):
            plan.last_w[rs] = ("d_wqk", plan.semcnt["d_wqk"])
        for rs in ("W1b_lo", "W1b_hi"):
            plan.last_w[rs] = ("d_wqk2", plan.semcnt["d_wqk2"])

        def gnw_scale(v):
            for cc in range(16):
                i = v.tensor_scalar(W3[:, cc * 1024:(cc + 1) * 1024],
                                    W3[:, cc * 1024:(cc + 1) * 1024],
                                    gnw4[:, cc % 4:cc % 4 + 1], None, ALU.mult, ALU.bypass)
            return i
        st(DVE, gnw_scale, reads=["gnw4"], writes=["W3"])

        if DEBUG:
            dbg_lg = nc.dram_tensor("dbg_lg", [128, 128], F32, kind="ExternalOutput").ap()
            dbg_pr = nc.dram_tensor("dbg_pr", [128, 128], F32, kind="ExternalOutput").ap()
            dbg_g = nc.dram_tensor("dbg_g", [128, 24], F32, kind="ExternalOutput").ap()
            dbg_r = nc.dram_tensor("dbg_r", [128, 1024], F32, kind="ExternalOutput").ap()
            dbg_on0 = nc.dram_tensor("dbg_on0", [128, 2048], BF16, kind="ExternalOutput").ap()
            dbg_xb = nc.dram_tensor("dbg_xb", [128, 1024], BF16, kind="ExternalOutput").ap()
            dbg_stv = nc.dram_tensor("dbg_stv", [128, 12], F32, kind="ExternalOutput").ap()
            dbg_rt = nc.dram_tensor("dbg_rt", [128, 32], F32, kind="ExternalOutput").ap()

        def load_x(t):
            xs = t % 2
            st(SP, (lambda e, t=t, xs=xs: e.dma_start(
                out=xf[:, xs * 1024:(xs + 1) * 1024], in_=x_h[t * 128:(t + 1) * 128, :])),
               writes=[f"xf{xs}"], dma=f"d_x{xs}")

        def cast_x(t):
            xs = t % 2
            st(ACT, (lambda e, xs=xs: e.activation(xb[:, :], xf[:, xs * 1024:(xs + 1) * 1024], AF.Copy)),
               reads=[f"xf{xs}"], writes=["xb"])

        load_x(0)
        cast_x(0)
        load_x(1)

        def emit_xtr():
            def xtr(e):
                for c in range(8):
                    i = e.transpose(T0[:, c * 128:(c + 1) * 128], xb[:, c * 128:(c + 1) * 128], idbf[:, :])
                return i
            st(PE, xtr, reads=["xb", "idbf"], writes=["T0"])
            st(ACT, lambda e: e.activation(xfm[:, :], T0[:, :], AF.Copy),
               reads=["T0"], writes=["xfm"])

        def emit_gkfront():
            def gk_mm(e):
                for c in range(8):
                    e.matmul(PCp[:, 0:4], xfm[:, c * 128:(c + 1) * 128], wgk[:, c * 4:(c + 1) * 4],
                             start=(c == 0), stop=False)
                return e.matmul(PCp[:, 0:4], onesr[:, :], bgkr[:, :], start=False, stop=True)
            st(PE, gk_mm, reads=["xfm", "wgk", "onesr", "bgkr", "cstm"], writes=["PC"])

            def g_chain(e):
                e.activation(t4[:, :], PCp[:, 0:4], AF.Exp, scale=-1.0)
                return e.activation(gsp[:, :], t4[:, :], AF.Ln, bias=onev[:, 0:1])
            st(ACT, g_chain, reads=["PC", "cstm"], writes=["gv"])

            def gc_mm(e):
                e.matmul(PCp[:, 4:8], utri[:, :], gsp[:, :], start=True, stop=True)
                return e.matmul(PCp[:, 8:12], onesn[:, :], gsp[:, :], start=True, stop=True)
            st(PE, gc_mm, reads=["utri", "onesn", "gv"], writes=["PC"])

            def exps(e):
                e.activation(eqv[:, :], PCp[:, 4:8], AF.Exp)
                e.activation(ekv[:, :], PCp[:, 4:8], AF.Exp, scale=-1.0)
                return e.activation(egtb[:, :], PCp[:, 8:12], AF.Exp)
            st(ACT, exps, reads=["PC"], writes=["gv"])

            st(DVE, lambda e: e.tensor_tensor(ekdv[:, :], ekv[:, :], egtb[:, :], ALU.mult),
               reads=["gv"], writes=["gv"])
        def emit_seg2(t):
                def xptr(e):
                    for c in range(8):
                        i = e.transpose(T0[:, c * 128:(c + 1) * 128], gsil[:, c * 128:(c + 1) * 128], idbf[:, :])
                    return i
                st(PE, xptr, reads=["gsil", "idbf"], writes=["T0"])
                st(ACT, (lambda e, t=t: e.activation(xpfm[:, t * 1024:(t + 1) * 1024], T0[:, :], AF.Copy)),
                   reads=["T0"], writes=["xpfm"])

                def router_mm(e, t=t):
                    for c in range(8):
                        i = e.matmul(PCp[:, 0:8], xpfm[:, t * 1024 + c * 128:t * 1024 + (c + 1) * 128],
                                     rw[:, c * 8:(c + 1) * 8], start=(c == 0), stop=(c == 7))
                    return i
                st(PE, router_mm, reads=["xpfm", "rw"], writes=["PC"])

                if DEBUG:
                    st(ACT, (lambda e: e.activation(mbb[:, :], PCp[:, 0:8], AF.Copy)),
                       reads=["PC"], writes=["rt"])
                    st(SP, (lambda e, t=t: e.dma_start(out=dbg_lg[:, t * 8:(t + 1) * 8], in_=mbb[:, :])),
                       reads=["rt"], dma="d_out0")

                st(ACT, lambda e: e.activation(pr[:, :], PCp[:, 0:8], AF.Copy),
                   reads=["PC"], writes=["rt"])

                def rtA(e):
                    sp_ = rta[:, 7:8]
                    e.tensor_scalar(mbb[:, :], pr[:, :], -1.0, None, ALU.mult, ALU.bypass)
                    e.memset(sp_, 0.0)
                    e.memset(sp_, 0.0)
                    e.tensor_tensor(msk[:, 0:4], mbb[:, 0:4], mbb[:, 4:8], ALU.min)
                    e.tensor_tensor(p2[:, 0:4], mbb[:, 0:4], mbb[:, 4:8], ALU.max)
                    e.tensor_tensor(msk[:, 4:6], msk[:, 0:2], msk[:, 2:4], ALU.min)
                    e.tensor_tensor(p2[:, 4:6], msk[:, 0:2], msk[:, 2:4], ALU.max)
                    e.memset(sp_, 0.0)
                    e.tensor_tensor(p2[:, 6:8], p2[:, 0:2], p2[:, 2:4], ALU.min)
                    e.memset(sp_, 0.0)
                    e.tensor_tensor(mbb[:, 0:2], p2[:, 4:6], p2[:, 6:8], ALU.min)
                    e.memset(sp_, 0.0)
                    e.tensor_tensor(msk[:, 1:2], msk[:, 4:5], msk[:, 5:6], ALU.max)
                    e.memset(sp_, 0.0)
                    e.tensor_tensor(mbb[:, 2:3], mbb[:, 0:1], mbb[:, 1:2], ALU.min)
                    e.memset(sp_, 0.0)
                    return e.tensor_tensor(rta[:, 2:3], msk[:, 1:2], mbb[:, 2:3], ALU.min)
                st(DVE, rtA, reads=["rt"], writes=["rt"])

                def rsg(e):
                    e.sign(p2[:, :], pr[:, :], bias=rta[:, 2:3])
                    return e.activation(pr[:, :], pr[:, :], AF.Exp)
                st(ACT, rsg, reads=["rt"], writes=["rt"])

                def rtB(e, t=t):
                    sp_ = rta[:, 7:8]
                    e.tensor_scalar(msk[:, :], p2[:, :], 1.0, 1.0, ALU.add, ALU.min)
                    e.tensor_tensor(mbb[:, 0:4], pr[:, 0:4], pr[:, 4:8], ALU.add)
                    e.memset(sp_, 0.0)
                    e.tensor_tensor(mbb[:, 4:6], mbb[:, 0:2], mbb[:, 2:4], ALU.add)
                    e.memset(sp_, 0.0)
                    e.tensor_tensor(rta[:, 3:4], mbb[:, 4:5], mbb[:, 5:6], ALU.add)
                    e.tensor_tensor(msk[:, :], msk[:, :], pr[:, :], ALU.mult)
                    e.reciprocal(rta[:, 4:5], rta[:, 3:4])
                    e.memset(sp_, 0.0)
                    e.memset(sp_, 0.0)
                    return e.tensor_scalar(csb[:, t * 8:(t + 1) * 8], msk[:, :], rta[:, 4:5], None,
                                           ALU.mult, ALU.bypass)
                st(DVE, rtB, reads=["rt"], writes=["rt", "c"])
                if DEBUG:
                    st(SP, (lambda e, t=t: e.dma_start(out=dbg_pr[:, t * 8:(t + 1) * 8], in_=pr[:, :])),
                       reads=["rt"], dma="d_out0")
                    if t == 5:
                        for i, sb in enumerate([p2, msk, mbb, rta]):
                            st(SP, (lambda e, i=i, sb=sb: e.dma_start(out=dbg_rt[:, i * 8:(i + 1) * 8], in_=sb[:, :])),
                               reads=["rt"], dma="d_out0")

        for t in range(NT):
            xslot = t % 2

            if t == 0:
                emit_xtr()
                emit_gkfront()

            if DEBUG and t == 0:
                for i, sb in enumerate([t4, gsp, eqv, ekv, ekdv, egtb]):
                    st(SP, (lambda e, i=i, sb=sb: e.dma_start(out=dbg_g[:, i * 4:(i + 1) * 4], in_=sb[:, :])),
                       reads=["gv"], dma="d_out0")
                st(SP, lambda e: e.dma_start(out=dbg_xb, in_=xb[:, :]), reads=["xb"], dma="d_out0")

            def q_mm(e):
                for c in range(8):
                    xc = xfm[:, c * 128:(c + 1) * 128]
                    e.matmul(PAp[:, 0:512], xc, W1[:, c * 1024:c * 1024 + 512], start=(c == 0), stop=(c == 7))
                    i = e.matmul(PAp[:, 512:1024], xc, W1[:, c * 1024 + 512:(c + 1) * 1024], start=(c == 0), stop=(c == 7))
                return i
            st(PE, q_mm, reads=["xfm", "W1a_lo", "W1a_hi"], writes=["PA0", "PA1"])

            def k_mm(e):
                for c in range(8):
                    xc = xfm[:, c * 128:(c + 1) * 128]
                    e.matmul(PBp[:, 0:512], xc, W1[:, 8192 + c * 1024:8192 + c * 1024 + 512], start=(c == 0), stop=(c == 7))
                    i = e.matmul(PBp[:, 512:1024], xc, W1[:, 8192 + c * 1024 + 512:8192 + (c + 1) * 1024], start=(c == 0), stop=(c == 7))
                return i
            st(PE, k_mm, reads=["xfm", "W1b_lo", "W1b_hi"], writes=["PB0", "PB1"])

            def q2_ev(e):
                for h in range(NHEAD):
                    i = e.tensor_scalar(q2[:, h * 256:(h + 1) * 256], PAp[:, h * 256:(h + 1) * 256],
                                        eqv[:, h:h + 1], 0.0625, ALU.mult, ALU.mult)
                return i
            st(DVE, q2_ev, reads=["PA0", "PA1", "gv"], writes=["q2"])

            def k1_ev(e):
                for h in range(NHEAD):
                    i = e.tensor_scalar(k1[:, h * 256:(h + 1) * 256], PBp[:, h * 256:(h + 1) * 256],
                                        ekv[:, h:h + 1], None, ALU.mult, ALU.bypass)
                return i
            st(DVE, k1_ev, reads=["PB0", "PB1", "gv"], writes=["k1"])

            def qtr(e):
                for c in range(8):
                    i = e.transpose(T1[:, c * 128:(c + 1) * 128], q2[:, c * 128:(c + 1) * 128], idbf[:, :])
                return i
            st(PE, qtr, reads=["q2", "idbf"], writes=["T1"])
            st(ACT, lambda e: e.activation(qfm[:, :], T1[:, :], AF.Copy), reads=["T1"], writes=["qfm"])

            def ktr(e):
                for c in range(8):
                    i = e.transpose(T0[:, c * 128:(c + 1) * 128], k1[:, c * 128:(c + 1) * 128], idbf[:, :])
                return i
            st(PE, ktr, reads=["k1", "idbf"], writes=["T0"])
            st(ACT, lambda e: e.activation(kfm[:, :], T0[:, :], AF.Copy), reads=["T0"], writes=["kfm"])

            def kd_ev(e):
                for h in range(NHEAD):
                    i = e.tensor_scalar(k1[:, h * 256:(h + 1) * 256], PBp[:, h * 256:(h + 1) * 256],
                                        ekdv[:, h:h + 1], None, ALU.mult, ALU.bypass)
                return i
            st(DVE, kd_ev, reads=["PB0", "PB1", "gv"], writes=["k1"])

            if t > 0:
                emit_seg2(t - 1)

            def v_mm(e, nt0, pb):
                for c in range(8):
                    xc = xfm[:, c * 128:(c + 1) * 128]
                    e.matmul(pb[:, 0:512], xc, W1[:, 16384 + c * 2048 + nt0 * 512:16384 + c * 2048 + (nt0 + 1) * 512],
                             start=(c == 0), stop=(c == 7))
                    i = e.matmul(pb[:, 512:1024], xc, W1[:, 16384 + c * 2048 + (nt0 + 1) * 512:16384 + c * 2048 + (nt0 + 2) * 512],
                                 start=(c == 0), stop=(c == 7))
                return i
            st(PE, (lambda e: v_mm(e, 0, PAp)), reads=["xfm", "W1c", "W1d"], writes=["PA0", "PA1"])
            st(DVE, lambda e: e.tensor_copy(vog[:, 0:1024], PAp[:, :]), reads=["PA0", "PA1"], writes=["vog"])
            st(PE, (lambda e: v_mm(e, 2, PBp)), reads=["xfm", "W1c", "W1d"], writes=["PB0", "PB1"])
            st(DVE, lambda e: e.tensor_copy(vog[:, 1024:2048], PBp[:, :]), reads=["PB0", "PB1"], writes=["vog"])

            def g_mm(e, nt0, pb):
                for c in range(8):
                    xc = xfm[:, c * 128:(c + 1) * 128]
                    e.matmul(pb[:, 0:512], xc, W2[:, c * 2048 + nt0 * 512:c * 2048 + (nt0 + 1) * 512],
                             start=(c == 0), stop=(c == 7))
                    i = e.matmul(pb[:, 512:1024], xc, W2[:, c * 2048 + (nt0 + 1) * 512:c * 2048 + (nt0 + 2) * 512],
                                 start=(c == 0), stop=(c == 7))
                return i
            st(PE, (lambda e: g_mm(e, 0, PAp)), reads=["xfm", "W2"], writes=["PA0", "PA1"])
            st(ACT, lambda e: e.activation(gsil[:, 0:1024], PAp[:, :], AF.Silu),
               reads=["PA0", "PA1"], writes=["gsil"])
            st(PE, (lambda e: g_mm(e, 2, PBp)), reads=["xfm", "W2"], writes=["PB0", "PB1"])
            st(ACT, lambda e: e.activation(gsil[:, 1024:2048], PBp[:, :], AF.Silu),
               reads=["PB0", "PB1"], writes=["gsil"])

            def at_pair(h):
                def at_mm(e, h=h):
                    e.matmul(PCp[:, 0:128], kfm[:, (2 * h) * 128:(2 * h + 1) * 128],
                             qfm[:, (2 * h) * 128:(2 * h + 1) * 128], start=True, stop=False)
                    return e.matmul(PCp[:, 0:128], kfm[:, (2 * h + 1) * 128:(2 * h + 2) * 128],
                                    qfm[:, (2 * h + 1) * 128:(2 * h + 2) * 128], start=False, stop=True)
                st(PE, at_mm, reads=["kfm", "qfm"], writes=["PC"])
                st(DVE, (lambda e: e.scalar_tensor_tensor(
                    At[:, 0:128], PCp[:, 0:128], 1.0, maskbf[:, :], ALU.mult, ALU.mult)),
                   reads=["PC", "maskbf"], writes=["At0"])

            at_pair(0)
            for h in range(NHEAD):
                OPp = PDp[:, :] if h % 2 == 0 else PBp[:, 0:512]
                ores = "PD" if h % 2 == 0 else "PB0"

                def o_mm(e, h=h, OPp=OPp):
                    e.matmul(OPp, At[:, 0:128], vog[:, h * 512:(h + 1) * 512],
                             start=True, stop=False)
                    e.matmul(OPp, qfm[:, (2 * h) * 128:(2 * h + 1) * 128],
                             S[:, (2 * h) * 512:(2 * h + 1) * 512], start=False, stop=False)
                    return e.matmul(OPp, qfm[:, (2 * h + 1) * 128:(2 * h + 2) * 128],
                                    S[:, (2 * h + 1) * 512:(2 * h + 2) * 512], start=False, stop=True)
                st(PE, o_mm, reads=["At0", "vog", "qfm", f"S{h}"], writes=[ores])

                if h + 1 < NHEAD:
                    at_pair(h + 1)

                def s_mm(e, h=h):
                    for cc in range(2):
                        bank = PAp[:, 0:512] if cc == 0 else PAp[:, 512:1024]
                        i = e.matmul(bank, k1[:, h * 256 + cc * 128:h * 256 + (cc + 1) * 128],
                                     vog[:, h * 512:(h + 1) * 512], start=True, stop=True)
                    return i
                st(PE, s_mm, reads=["k1", "vog"], writes=["PA0", "PA1"])

                def rms(e, h=h, OPp=OPp):
                    e.activation(q2[:, 0:512], OPp, AF.Square, accum_out=rhv[:, 0:1])
                    e.activation(rhv[:, 1:2], rhv[:, 0:1], AF.Ln, bias=epsv[:, 0:1], scale=1.0 / 512)
                    return e.activation(rhv[:, 2:3], rhv[:, 1:2], AF.Exp, scale=-0.5)
                st(ACT, rms, reads=[ores, "cstm"], writes=["rh", "q2"])

                st(DVE, (lambda e, h=h, OPp=OPp: e.tensor_scalar(
                    on_[:, h * 512:(h + 1) * 512], OPp, rhv[:, 2:3], None, ALU.mult, ALU.bypass)),
                   reads=[ores, "rh"], writes=["on_"])

                def s_ev(e, h=h):
                    e.scalar_tensor_tensor(S[:, (2 * h) * 512:(2 * h + 1) * 512],
                                           S[:, (2 * h) * 512:(2 * h + 1) * 512], egtb[:, h:h + 1],
                                           PAp[:, 0:512], ALU.mult, ALU.add)
                    return e.scalar_tensor_tensor(S[:, (2 * h + 1) * 512:(2 * h + 2) * 512],
                                                  S[:, (2 * h + 1) * 512:(2 * h + 2) * 512], egtb[:, h:h + 1],
                                                  PAp[:, 512:1024], ALU.mult, ALU.add)
                st(DVE, s_ev, reads=["PA0", "PA1", "gv"], writes=[f"S{h}"])

            st(DVE, lambda e: e.tensor_tensor(gsil[:, :], gsil[:, :], on_[:, :], ALU.mult),
               reads=["gsil", "on_"], writes=["gsil"])

            def ogtr(e, half):
                tb = Tb[half]
                for c in range(8):
                    i = e.transpose(tb[:, c * 128:(c + 1) * 128],
                                    gsil[:, half * 1024 + c * 128:half * 1024 + (c + 1) * 128], idbf[:, :])
                return i
            st(PE, (lambda e: ogtr(e, 0)), reads=["gsil", "idbf"], writes=["T0"])
            st(ACT, lambda e: e.activation(vog[:, 0:1024], T0[:, :], AF.Copy), reads=["T0"], writes=["vog"])
            st(PE, (lambda e: ogtr(e, 1)), reads=["gsil", "idbf"], writes=["T1"])
            st(ACT, lambda e: e.activation(vog[:, 1024:2048], T1[:, :], AF.Copy), reads=["T1"], writes=["vog"])

            if t + 1 < NT:
                cast_x(t + 1)
                emit_xtr()
                emit_gkfront()

            def wo_mm(e):
                for cc in range(16):
                    oc = vog[:, cc * 128:(cc + 1) * 128]
                    e.matmul(PAp[:, 0:512], oc, W3[:, cc * 1024:cc * 1024 + 512], start=(cc == 0), stop=(cc == 15))
                    i = e.matmul(PAp[:, 512:1024], oc, W3[:, cc * 1024 + 512:(cc + 1) * 1024], start=(cc == 0), stop=(cc == 15))
                return i
            st(PE, wo_mm, reads=["vog", "W3"], writes=["PA0", "PA1"])

            def resid(e, xs=xslot):
                return e.scalar_tensor_tensor(r_[:, :], PAp[:, :], 1.0,
                                              xf[:, xs * 1024:(xs + 1) * 1024],
                                              ALU.mult, ALU.add, accum_out=stv[:, 0:1])
            st(DVE, resid, reads=["PA0", "PA1", f"xf{xslot}"], writes=["r_", "stt"])
            if t + 2 < NT:
                load_x(t + 2)

            def sqc(e):
                e.activation(q2[:, :], r_[:, :], AF.Square, accum_out=stv[:, 8:9])
                return e.activation(stv[:, 6:7], stv[:, 0:1], AF.Square, scale=1.0 / 1024)
            st(ACT, sqc, reads=["r_", "stt"], writes=["stt", "q2"])

            def dvar(e):
                e.tensor_scalar(stv[:, 4:5], stv[:, 0:1], -1.0 / 1024, None, ALU.mult, ALU.bypass)
                return e.scalar_tensor_tensor(stv[:, 7:8], stv[:, 8:9], 1.0 / 1024, stv[:, 6:7],
                                              ALU.mult, ALU.subtract)
            st(DVE, dvar, reads=["stt"], writes=["stt"])

            def rsc(e):
                e.activation(stv[:, 9:10], stv[:, 7:8], AF.Ln, bias=epsv[:, 0:1])
                return e.activation(stv[:, 10:11], stv[:, 9:10], AF.Exp, scale=-0.5)
            st(ACT, rsc, reads=["stt", "cstm"], writes=["stt"])

            st(DVE, lambda e: e.tensor_scalar(gsil[:, 0:1024], r_[:, :], stv[:, 4:5], stv[:, 10:11],
                                              ALU.add, ALU.mult),
               reads=["r_", "stt"], writes=["gsil"])

            if DEBUG and t == 0:
                st(SP, lambda e: e.dma_start(out=dbg_on0, in_=on_[:, :]), reads=["on_"], dma="d_out0")
                st(SP, lambda e: e.dma_start(out=dbg_r, in_=r_[:, :]), reads=["r_"], dma="d_out0")
                st(SP, lambda e: e.dma_start(out=dbg_stv, in_=stv[:, :]), reads=["stt"], dma="d_out0")

        emit_seg2(NT - 1)

        st(POOL, lambda e: e.dma_start(out=W2[0:1, 0:8192], in_=bin_h, max_dma_last_dim=8192),
           writes=["W2"], dma="d_bin")
        st(POOL, lambda e: e.dma_start(out=W2[0:1, 8192:16384], in_=bout_h, max_dma_last_dim=8192),
           writes=["W2"], dma="d_bin")

        SLOT = [(0, ["W1a_lo", "W1a_hi", "W1b_lo", "W1b_hi"]), (16384, ["W1c", "W1d"])]

        def load_expert(ee):
            base, res = SLOT[ee % 2]
            half = len(res) // 2
            st(POOL, (lambda e, ee=ee, base=base: e.dma_start(
                out=W1[:, base:base + 8192], in_=win_h[ee], max_dma_last_dim=8192)),
               writes=res[:half], dma=f"d_e{ee % 2}a")
            st(POOL, (lambda e, ee=ee, base=base: e.dma_start(
                out=W1[:, base + 8192:base + 16384], in_=wout_h[ee], max_dma_last_dim=8192)),
               writes=res[half:], dma=f"d_e{ee % 2}b")

        load_expert(0)
        load_expert(1)

        PAIRS = [(e_i, t) for e_i in range(NE) for t in range(NT)]

        def b_hmm(i):
            e_i, t = PAIRS[i]
            base, slot_res = SLOT[e_i % 2]
            HPp = PAp if i % 2 == 0 else PBp
            hres = ["PA0", "PA1"] if i % 2 == 0 else ["PB0", "PB1"]
            htm = q2 if i % 2 == 0 else xfm
            hmr = "q2" if i % 2 == 0 else "xfm"

            def h_mm(e, base=base, t=t, e_i=e_i, HPp=HPp):
                for c in range(8):
                    xc = xpfm[:, t * 1024 + c * 128:t * 1024 + (c + 1) * 128]
                    e.matmul(HPp[:, 0:512], xc, W1[:, base + c * 1024:base + c * 1024 + 512],
                             start=(c == 0), stop=False)
                    e.matmul(HPp[:, 512:1024], xc, W1[:, base + c * 1024 + 512:base + (c + 1) * 1024],
                             start=(c == 0), stop=False)
                e.matmul(HPp[:, 0:512], onesr[:, :], W2[0:1, e_i * 1024:e_i * 1024 + 512],
                         start=False, stop=True)
                return e.matmul(HPp[:, 512:1024], onesr[:, :], W2[0:1, e_i * 1024 + 512:(e_i + 1) * 1024],
                                start=False, stop=True)
            st(PE, h_mm, reads=["xpfm"] + slot_res[:len(slot_res) // 2] + ["W2", "onesr", "cstm"], writes=hres)

            def gelu(e, HPp=HPp, htm=htm):
                e.activation(htm[:, 0:512], HPp[:, 0:512], AF.Gelu_apprx_tanh)
                return e.activation(htm[:, 512:1024], HPp[:, 512:1024], AF.Gelu_apprx_tanh)
            st(ACT, gelu, reads=hres, writes=[hmr])

        b_hmm(0)
        b_hmm(1)
        for i, (e_i, t) in enumerate(PAIRS):
            base, slot_res = SLOT[e_i % 2]
            tp = i % 2

            htm = q2 if i % 2 == 0 else xfm
            hmr = "q2" if i % 2 == 0 else "xfm"

            def htr(e, tp=tp, htm=htm):
                tb = Tb[tp]
                for c in range(8):
                    i2 = e.transpose(tb[:, c * 128:(c + 1) * 128], htm[:, c * 128:(c + 1) * 128], idbf[:, :])
                return i2
            st(PE, htr, reads=[hmr, "idbf"], writes=[f"T{tp}"])
            st(ACT, (lambda e, tp=tp: e.activation(k1[:, :], Tb[tp][:, :], AF.Copy)),
               reads=[f"T{tp}"], writes=["k1"])

            if i + 2 < len(PAIRS):
                b_hmm(i + 2)

            if e_i == 0:
                st(DVE, (lambda e, t=t: e.memset(W3[:, t * 1024:(t + 1) * 1024], 0.0)),
                   writes=["W3"])

            def y_mm(e, base=base, e_i=e_i):
                for c in range(8):
                    hc = k1[:, c * 128:(c + 1) * 128]
                    e.matmul(PCp[:, :], hc, W1[:, base + 8192 + c * 1024:base + 8192 + c * 1024 + 512],
                             start=(c == 0), stop=False)
                    e.matmul(PDp[:, :], hc, W1[:, base + 8192 + c * 1024 + 512:base + 8192 + (c + 1) * 1024],
                             start=(c == 0), stop=False)
                e.matmul(PCp[:, :], onesr[:, :], W2[0:1, 8192 + e_i * 1024:8192 + e_i * 1024 + 512],
                         start=False, stop=True)
                return e.matmul(PDp[:, :], onesr[:, :], W2[0:1, 8192 + e_i * 1024 + 512:8192 + (e_i + 1) * 1024],
                                start=False, stop=True)
            st(PE, y_mm, reads=["k1"] + slot_res[len(slot_res) // 2:] + ["W2", "onesr", "cstm"], writes=["PC", "PD"])

            def y_acc(e, t=t, e_i=e_i):
                e.scalar_tensor_tensor(W3[:, t * 1024:t * 1024 + 512], PCp[:, :],
                                       csb[:, t * 8 + e_i:t * 8 + e_i + 1],
                                       W3[:, t * 1024:t * 1024 + 512], ALU.mult, ALU.add)
                return e.scalar_tensor_tensor(W3[:, t * 1024 + 512:(t + 1) * 1024], PDp[:, :],
                                              csb[:, t * 8 + e_i:t * 8 + e_i + 1],
                                              W3[:, t * 1024 + 512:(t + 1) * 1024], ALU.mult, ALU.add)
            st(DVE, y_acc, reads=["PC", "PD", "c", "W3"], writes=["W3"])

            if t == NT - 1 and e_i + 2 < NE:
                load_expert(e_i + 2)

        def c_tr(t):
            tp = t % 2
            xtm = q2 if tp == 0 else k1
            xres = "q2" if tp == 0 else "k1"

            def ctr(e, t=t, tp=tp):
                for c in range(8):
                    i = e.transpose(Tb[tp][:, c * 128:(c + 1) * 128],
                                    xpfm[:, t * 1024 + c * 128:t * 1024 + (c + 1) * 128], idbf[:, :])
                return i
            st(PE, ctr, reads=["xpfm", "idbf"], writes=[f"T{tp}"])
            st(ACT, (lambda e, tp=tp, xtm=xtm: e.activation(xtm[:, :], Tb[tp][:, :], AF.Copy)),
               reads=[f"T{tp}"], writes=[xres])

        c_tr(0)
        c_tr(1)
        for t in range(NT):
            tp = t % 2
            xtm = q2 if tp == 0 else k1
            xres = "q2" if tp == 0 else "k1"

            def cr(e, t=t, xtm=xtm):
                return e.scalar_tensor_tensor(r_[:, :], W3[:, t * 1024:(t + 1) * 1024], 1.0, xtm[:, :],
                                              ALU.mult, ALU.add, accum_out=stv[:, 2:3])
            st(DVE, cr, reads=["W3", xres], writes=["r_", "stt"])

            if t + 2 < NT:
                c_tr(t + 2)

            def csqc(e):
                e.activation(kfm[:, :], r_[:, :], AF.Square, accum_out=stv[:, 8:9])
                return e.activation(stv[:, 6:7], stv[:, 2:3], AF.Square, scale=1.0 / 1024)
            st(ACT, csqc, reads=["r_", "stt"], writes=["stt", "kfm"])

            def cdvar(e):
                e.tensor_scalar(stv[:, 3:4], stv[:, 2:3], -1.0 / 1024, None, ALU.mult, ALU.bypass)
                return e.scalar_tensor_tensor(stv[:, 7:8], stv[:, 8:9], 1.0 / 1024, stv[:, 6:7],
                                              ALU.mult, ALU.subtract)
            st(DVE, cdvar, reads=["stt"], writes=["stt"])

            def crsc(e):
                e.activation(stv[:, 9:10], stv[:, 7:8], AF.Ln, bias=epsv[:, 0:1])
                return e.activation(stv[:, 10:11], stv[:, 9:10], AF.Exp, scale=-0.5)
            st(ACT, crsc, reads=["stt", "cstm"], writes=["stt"])

            xs = t % 2
            st(DVE, (lambda e, xs=xs: e.tensor_scalar(xf[:, xs * 1024:(xs + 1) * 1024], r_[:, :],
                                                      stv[:, 3:4], stv[:, 10:11], ALU.add, ALU.mult)),
               reads=["r_", "stt"], writes=[f"xf{xs}"])
            st(SP, (lambda e, t=t, xs=xs: e.dma_start(
                out=out_h[t * 128:(t + 1) * 128, :], in_=xf[:, xs * 1024:(xs + 1) * 1024])),
               reads=[f"xf{xs}"], dma=f"d_out{xs}")

        if DEBUG:
            dxp = nc.dram_tensor("dbg_xp", [128, 16384], BF16, kind="ExternalOutput").ap()
            dc = nc.dram_tensor("dbg_c", [128, 128], F32, kind="ExternalOutput").ap()
            dy = nc.dram_tensor("dbg_y", [128, 16384], BF16, kind="ExternalOutput").ap()
            don = nc.dram_tensor("dbg_on", [128, 2048], BF16, kind="ExternalOutput").ap()
            st(SP, lambda e: e.dma_start(out=dxp, in_=xpfm[:, :]), reads=["xpfm"], dma="d_out0")
            st(SP, lambda e: e.dma_start(out=dc, in_=csb[:, :]), reads=["c"], dma="d_out0")
            st(SP, lambda e: e.dma_start(out=dy, in_=W3[:, :]), reads=["W3"], dma="d_out0")
            st(SP, lambda e: e.dma_start(out=don, in_=on_[:, :]), reads=["on_"], dma="d_out0")

        block = E(nc.Block())
        eng_map = {}

        def runner(engname):
            def run(eng):
                for (en, waits, emit, comp) in plan.stages:
                    if en != engname:
                        continue
                    for semname, val in waits:
                        eng.wait_ge(sems[semname], val)
                    inst = emit(eng)
                    inst.then_inc(sems[comp[0]], 16 if comp[0].startswith("d_") else 1)
                if engname == SP:
                    eng.wait_ge(sems["d_out0"], plan.semcnt.get("d_out0", 0))
                    eng.wait_ge(sems["d_out1"], plan.semcnt.get("d_out1", 0))
            return run

        block.tensor(runner(PE))
        block.vector(runner(DVE))
        block.scalar(runner(ACT))
        block.sync(runner(SP))
        block.gpsimd(runner(POOL))

    return nc


def _lay8(w):
    n = w.shape[1]
    return np.ascontiguousarray(w.reshape(8, 128, n).transpose(1, 0, 2).reshape(128, 8 * n))


def _lay16(w):
    n = w.shape[1]
    return np.ascontiguousarray(w.reshape(16, 128, n).transpose(1, 0, 2).reshape(128, 16 * n))


def kernel(L0_attn_x, L0_attn_Wq, L0_attn_Wk, L0_attn_Wv, L0_attn_Wg, L0_attn_Wgk,
           L0_attn_bgk, L0_attn_Wo, L0_attn_gnorm_w, L0_ffn_router_w, L0_ffn_w_in,
           L0_ffn_b_in, L0_ffn_w_out, L0_ffn_b_out):
    x = np.asarray(L0_attn_x, dtype=np.float32)
    shared = {
        "wq": _lay8(np.asarray(L0_attn_Wq, np.float32)),
        "wk": _lay8(np.asarray(L0_attn_Wk, np.float32)),
        "wv": _lay8(np.asarray(L0_attn_Wv, np.float32)),
        "wg": _lay8(np.asarray(L0_attn_Wg, np.float32)),
        "wo": _lay16(np.asarray(L0_attn_Wo, np.float32)),
        "wgk": _lay8(np.asarray(L0_attn_Wgk, np.float32)),
        "rw": _lay8(np.asarray(L0_ffn_router_w, np.float32)),
        "bgk": np.ascontiguousarray(np.asarray(L0_attn_bgk, np.float32).reshape(1, 4)),
        "bin": np.ascontiguousarray(np.asarray(L0_ffn_b_in, np.float32).reshape(1, 8192)),
        "bout": np.ascontiguousarray(np.asarray(L0_ffn_b_out, np.float32).reshape(1, 8192)),
        "gnw4": np.ascontiguousarray(np.asarray(L0_attn_gnorm_w, np.float32).reshape(4, 128).T),
        "idbf": np.eye(128).astype(ml_dtypes.bfloat16),
        "utri": np.ascontiguousarray(np.triu(np.ones((128, 128), np.float32)) * (-1.0 / 16.0)),
        "onesn": np.full((128, 128), -1.0 / 16.0, np.float32),
        "maskbf": np.triu(np.ones((128, 128))).astype(ml_dtypes.bfloat16),
    }
    w_in = np.asarray(L0_ffn_w_in, np.float32)
    w_out = np.asarray(L0_ffn_w_out, np.float32)
    for e in range(NE):
        shared[f"win{e}"] = _lay8(w_in[e])
        shared[f"wout{e}"] = _lay8(w_out[e])

    nc = build_nc()
    in_maps = []
    for c in range(8):
        m = dict(shared)
        m["x"] = np.ascontiguousarray(x[c])
        in_maps.append(m)
    res = run_bass_kernel_spmd(nc, in_maps, core_ids=list(range(8)))
    out = np.stack([np.asarray(res.results[c]["out"]) for c in range(8)], axis=0)
    return out.astype(np.float32)


# version 16
# speedup vs baseline: 1.0664x; 1.0195x over previous
import contextlib
import numpy as np
import ml_dtypes

import concourse.bass as bass
import concourse.mybir as mybir
from concourse.bass_utils import run_bass_kernel_spmd

F32 = mybir.dt.float32
BF16 = mybir.dt.bfloat16
AF = mybir.ActivationFunctionType
ALU = mybir.AluOpType
AX = mybir.AxisListType

NT = 16
NHEAD = 4
DK = 256
DV = 512
D = 1024
NE = 8
EPS = 1e-5

PE, DVE, ACT, SP, POOL = "pe", "dve", "act", "sp", "pool"
ENG_SEM = {PE: "s_pe", DVE: "s_dve", ACT: "s_act", POOL: "s_pool"}


class Plan:
    def __init__(self):
        self.stages = []
        self.semcnt = {}
        self.last_w = {}
        self.readers = {}
        self.waited = {e: {} for e in (PE, DVE, ACT, SP, POOL)}

    def stage(self, eng, emit, reads=(), writes=(), dma=None):
        waits = {}
        own = ENG_SEM.get(eng)

        def add(dep):
            if dep is None:
                return
            sem, val = dep
            if sem == own:
                return
            if self.waited[eng].get(sem, -1) >= val:
                return
            if waits.get(sem, -1) < val:
                waits[sem] = val

        for rs in reads:
            add(self.last_w.get(rs))
        for ws in writes:
            add(self.last_w.get(ws))
            for sem, val in self.readers.get(ws, {}).items():
                add((sem, val))
        for s, v in waits.items():
            self.waited[eng][s] = v
        if dma is not None:
            self.semcnt[dma] = self.semcnt.get(dma, 0) + 16
            comp = (dma, self.semcnt[dma])
        else:
            s = ENG_SEM[eng]
            self.semcnt[s] = self.semcnt.get(s, 0) + 1
            comp = (s, self.semcnt[s])
        for rs in reads:
            d = self.readers.setdefault(rs, {})
            if d.get(comp[0], -1) < comp[1]:
                d[comp[0]] = comp[1]
        for ws in writes:
            self.last_w[ws] = comp
            self.readers[ws] = {}
        self.stages.append((eng, sorted(waits.items()), emit, comp))


DEBUG = False


def build_nc():
    nc = bass.Bass("TRN2", target_bir_lowering=False)

    dt = {}
    def din(name, shape, dtype=F32):
        dt[name] = nc.dram_tensor(name, shape, dtype, kind="ExternalInput").ap()
        return dt[name]

    x_h = din("x", [2048, 1024])
    wq_h = din("wq", [128, 8192])
    wk_h = din("wk", [128, 8192])
    wv_h = din("wv", [128, 16384])
    wg_h = din("wg", [128, 16384])
    wo_h = din("wo", [128, 16384])
    wgk_h = din("wgk", [128, 32])
    rw_h = din("rw", [128, 64])
    bgk_h = din("bgk", [1, 4])
    bin_h = din("bin", [1, 8192])
    bout_h = din("bout", [1, 8192])
    gnw4_h = din("gnw4", [128, 4])
    idbf_h = din("idbf", [128, 128], BF16)
    utri_h = din("utri", [128, 128])
    onesn_h = din("onesn", [128, 128])
    maskbf_h = din("maskbf", [128, 128], BF16)
    win_h = [din(f"win{e}", [128, 8192]) for e in range(NE)]
    wout_h = [din(f"wout{e}", [128, 8192]) for e in range(NE)]
    out_h = nc.dram_tensor("out", [2048, 1024], F32, kind="ExternalOutput").ap()

    plan = Plan()

    with contextlib.ExitStack() as ctx:
        E = ctx.enter_context

        W1 = E(nc.sbuf_tensor("W1", [128, 32768], BF16))
        W2 = E(nc.sbuf_tensor("W2", [128, 16384], BF16))
        W3 = E(nc.sbuf_tensor("W3", [128, 16384], BF16))
        xpfm = E(nc.sbuf_tensor("xpfm", [128, 16384], BF16))
        S = E(nc.sbuf_tensor("S", [128, 4096], BF16))
        csb = E(nc.sbuf_tensor("csb", [128, 128], F32))
        xf = E(nc.sbuf_tensor("xf", [128, 2048], F32))
        xb = E(nc.sbuf_tensor("xb", [128, 1024], BF16))
        xfm = E(nc.sbuf_tensor("xfm", [128, 1024], BF16))
        q2 = E(nc.sbuf_tensor("q2", [128, 1024], BF16))
        k1 = E(nc.sbuf_tensor("k1", [128, 1024], BF16))
        qfm = E(nc.sbuf_tensor("qfm", [128, 1024], BF16))
        kfm = E(nc.sbuf_tensor("kfm", [128, 1024], BF16))
        vog = E(nc.sbuf_tensor("vog", [128, 2048], BF16))
        gsil = E(nc.sbuf_tensor("gsil", [128, 2048], BF16))
        on_ = E(nc.sbuf_tensor("on_", [128, 2048], BF16))
        r_ = E(nc.sbuf_tensor("r_", [128, 1024], F32))
        diag = E(nc.sbuf_tensor("diag", [128, 128], BF16))
        At = E(nc.sbuf_tensor("At", [128, 128], BF16))
        idbf = E(nc.sbuf_tensor("idbf_s", [128, 128], BF16))
        utri = E(nc.sbuf_tensor("utri_s", [128, 128], F32))
        onesn = E(nc.sbuf_tensor("onesn_s", [128, 128], F32))
        maskbf = E(nc.sbuf_tensor("maskbf_s", [128, 128], BF16))
        gnw4 = E(nc.sbuf_tensor("gnw4_s", [128, 4], F32))
        wgk = E(nc.sbuf_tensor("wgk_s", [128, 32], BF16))
        rw = E(nc.sbuf_tensor("rw_s", [128, 64], BF16))
        bgkr = E(nc.sbuf_tensor("bgkr_s", [1, 4], BF16))
        onesr = E(nc.sbuf_tensor("onesr", [1, 128], BF16))
        epsv = E(nc.sbuf_tensor("epsv", [128, 1], F32))
        onev = E(nc.sbuf_tensor("onev", [128, 1], F32))
        t4 = E(nc.sbuf_tensor("t4", [128, 4], F32))
        gsp = E(nc.sbuf_tensor("gsp", [128, 4], F32))
        eqv = E(nc.sbuf_tensor("eqv", [128, 4], F32))
        ekv = E(nc.sbuf_tensor("ekv", [128, 4], F32))
        ekdv = E(nc.sbuf_tensor("ekdv", [128, 4], F32))
        egtb = E(nc.sbuf_tensor("egtb", [128, 4], F32))
        stv = E(nc.sbuf_tensor("stv", [128, 12], F32))
        rhv = E(nc.sbuf_tensor("rhv", [128, 4], F32))
        pr = E(nc.sbuf_tensor("pr", [128, 8], F32))
        mbb = E(nc.sbuf_tensor("mbb", [128, 8], F32))
        msk = E(nc.sbuf_tensor("msk", [128, 8], F32))
        p2 = E(nc.sbuf_tensor("p2", [128, 8], F32))
        rta = E(nc.sbuf_tensor("rta", [128, 8], F32))

        PAp = E(nc.psum_tensor("PA", [128, 1024], F32))
        PBp = E(nc.psum_tensor("PB", [128, 1024], F32))
        PCp = E(nc.psum_tensor("PC", [128, 512], F32))
        PDp = E(nc.psum_tensor("PD", [128, 512], F32))
        T0 = E(nc.psum_tensor("T0", [128, 1024], BF16))
        T1 = E(nc.psum_tensor("T1", [128, 1024], BF16))
        Tb = [T0, T1]

        sem_names = ["s_pe", "s_dve", "s_act", "s_pool", "d_c", "d_sc", "d_wqk", "d_wqk2", "d_wv",
                     "d_wg", "d_wo", "d_bin", "d_x0", "d_x1", "d_out0", "d_out1",
                     "d_e0a", "d_e0b", "d_e1a", "d_e1b"]
        sems = {n: E(nc.semaphore(n)) for n in sem_names}

        st = plan.stage

        def mset(v):
            v.memset(S[:, :], 0.0)
            v.memset(epsv[:, :], EPS)
            v.memset(onev[:, :], 1.0)
            return v.memset(onesr[:, :], 1.0)
        st(DVE, mset, writes=["S0", "S1", "S2", "S3", "cstm", "onesr"])

        for name, hbm, sb in [("idbf", idbf_h, idbf), ("utri", utri_h, utri),
                              ("onesn", onesn_h, onesn), ("maskbf", maskbf_h, maskbf),
                              ("gnw4", gnw4_h, gnw4)]:
            st(SP, (lambda e, h=hbm, s=sb: e.dma_start(out=s[:, :], in_=h)),
               writes=[name], dma="d_c")
        for name, hbm, sb in [("wgk", wgk_h, wgk), ("rw", rw_h, rw), ("bgkr", bgk_h, bgkr)]:
            st(POOL, (lambda e, h=hbm, s=sb: e.dma_start(out=s[:, :], in_=h)),
               writes=[name], dma="d_sc")
        st(POOL, lambda e: e.dma_start(out=W1[:, 0:8192], in_=wq_h, max_dma_last_dim=8192),
           writes=["W1a_lo", "W1a_hi"], dma="d_wqk")
        st(POOL, lambda e: e.dma_start(out=W1[:, 8192:16384], in_=wk_h, max_dma_last_dim=8192),
           writes=["W1b_lo", "W1b_hi"], dma="d_wqk2")
        st(POOL, lambda e: e.dma_start(out=W1[:, 16384:32768], in_=wv_h, max_dma_last_dim=8192),
           writes=["W1c", "W1d"], dma="d_wv")
        st(POOL, lambda e: e.dma_start(out=W2[:, :], in_=wg_h, max_dma_last_dim=8192),
           writes=["W2"], dma="d_wg")
        st(POOL, lambda e: e.dma_start(out=W3[:, :], in_=wo_h, max_dma_last_dim=8192),
           writes=["W3"], dma="d_wo")

        for rs in ("idbf", "utri", "onesn", "maskbf", "gnw4"):
            plan.last_w[rs] = ("d_c", plan.semcnt["d_c"])
        for rs in ("wgk", "rw", "bgkr"):
            plan.last_w[rs] = ("d_sc", plan.semcnt["d_sc"])
        for rs in ("W1a_lo", "W1a_hi"):
            plan.last_w[rs] = ("d_wqk", plan.semcnt["d_wqk"])
        for rs in ("W1b_lo", "W1b_hi"):
            plan.last_w[rs] = ("d_wqk2", plan.semcnt["d_wqk2"])

        def gnw_scale(v):
            for cc in range(16):
                i = v.tensor_scalar(W3[:, cc * 1024:(cc + 1) * 1024],
                                    W3[:, cc * 1024:(cc + 1) * 1024],
                                    gnw4[:, cc % 4:cc % 4 + 1], None, ALU.mult, ALU.bypass)
            return i
        st(DVE, gnw_scale, reads=["gnw4"], writes=["W3"])

        if DEBUG:
            dbg_lg = nc.dram_tensor("dbg_lg", [128, 128], F32, kind="ExternalOutput").ap()
            dbg_pr = nc.dram_tensor("dbg_pr", [128, 128], F32, kind="ExternalOutput").ap()
            dbg_g = nc.dram_tensor("dbg_g", [128, 24], F32, kind="ExternalOutput").ap()
            dbg_r = nc.dram_tensor("dbg_r", [128, 1024], F32, kind="ExternalOutput").ap()
            dbg_on0 = nc.dram_tensor("dbg_on0", [128, 2048], BF16, kind="ExternalOutput").ap()
            dbg_xb = nc.dram_tensor("dbg_xb", [128, 1024], BF16, kind="ExternalOutput").ap()
            dbg_stv = nc.dram_tensor("dbg_stv", [128, 12], F32, kind="ExternalOutput").ap()
            dbg_rt = nc.dram_tensor("dbg_rt", [128, 32], F32, kind="ExternalOutput").ap()

        def load_x(t):
            xs = t % 2
            st(SP, (lambda e, t=t, xs=xs: e.dma_start(
                out=xf[:, xs * 1024:(xs + 1) * 1024], in_=x_h[t * 128:(t + 1) * 128, :])),
               writes=[f"xf{xs}"], dma=f"d_x{xs}")

        def cast_x(t):
            xs = t % 2
            st(ACT, (lambda e, xs=xs: e.activation(xb[:, :], xf[:, xs * 1024:(xs + 1) * 1024], AF.Copy)),
               reads=[f"xf{xs}"], writes=["xb"])

        load_x(0)
        cast_x(0)
        load_x(1)

        def emit_xtr():
            def xtr(e):
                for c in range(8):
                    i = e.transpose(T0[:, c * 128:(c + 1) * 128], xb[:, c * 128:(c + 1) * 128], idbf[:, :])
                return i
            st(PE, xtr, reads=["xb", "idbf"], writes=["T0"])
            st(ACT, lambda e: e.activation(xfm[:, :], T0[:, :], AF.Copy),
               reads=["T0"], writes=["xfm"])

        def emit_gkfront():
            def gk_mm(e):
                for c in range(8):
                    e.matmul(PBp[:, 512:516], xfm[:, c * 128:(c + 1) * 128], wgk[:, c * 4:(c + 1) * 4],
                             start=(c == 0), stop=False)
                return e.matmul(PBp[:, 512:516], onesr[:, :], bgkr[:, :], start=False, stop=True)
            st(PE, gk_mm, reads=["xfm", "wgk", "onesr", "bgkr", "cstm"], writes=["PB1"])

            def g_chain(e):
                e.activation(t4[:, :], PBp[:, 512:516], AF.Exp, scale=-1.0)
                return e.activation(gsp[:, :], t4[:, :], AF.Ln, bias=onev[:, 0:1])
            st(ACT, g_chain, reads=["PB1", "cstm"], writes=["gv"])

            def gc_mm(e):
                e.matmul(PBp[:, 516:520], utri[:, :], gsp[:, :], start=True, stop=True)
                return e.matmul(PBp[:, 520:524], onesn[:, :], gsp[:, :], start=True, stop=True)
            st(PE, gc_mm, reads=["utri", "onesn", "gv"], writes=["PB1"])

            def exps(e):
                e.activation(eqv[:, :], PBp[:, 516:520], AF.Exp)
                e.activation(ekv[:, :], PBp[:, 516:520], AF.Exp, scale=-1.0)
                return e.activation(egtb[:, :], PBp[:, 520:524], AF.Exp)
            st(ACT, exps, reads=["PB1"], writes=["gv"])

            st(DVE, lambda e: e.tensor_tensor(ekdv[:, :], ekv[:, :], egtb[:, :], ALU.mult),
               reads=["gv"], writes=["gv"])
        def emit_seg2(t):
                def xptr(e):
                    for c in range(8):
                        i = e.transpose(T0[:, c * 128:(c + 1) * 128], gsil[:, c * 128:(c + 1) * 128], idbf[:, :])
                    return i
                st(PE, xptr, reads=["gsil", "idbf"], writes=["T0"])
                st(ACT, (lambda e, t=t: e.activation(xpfm[:, t * 1024:(t + 1) * 1024], T0[:, :], AF.Copy)),
                   reads=["T0"], writes=["xpfm"])

                def router_mm(e, t=t):
                    for c in range(8):
                        i = e.matmul(PCp[:, 0:8], xpfm[:, t * 1024 + c * 128:t * 1024 + (c + 1) * 128],
                                     rw[:, c * 8:(c + 1) * 8], start=(c == 0), stop=(c == 7))
                    return i
                st(PE, router_mm, reads=["xpfm", "rw"], writes=["PC"])

                if DEBUG:
                    st(ACT, (lambda e: e.activation(mbb[:, :], PCp[:, 0:8], AF.Copy)),
                       reads=["PC"], writes=["rt"])
                    st(SP, (lambda e, t=t: e.dma_start(out=dbg_lg[:, t * 8:(t + 1) * 8], in_=mbb[:, :])),
                       reads=["rt"], dma="d_out0")

                st(ACT, lambda e: e.activation(pr[:, :], PCp[:, 0:8], AF.Copy),
                   reads=["PC"], writes=["rt"])

                def rtA(e):
                    sp_ = rta[:, 7:8]
                    e.tensor_scalar(mbb[:, :], pr[:, :], -1.0, None, ALU.mult, ALU.bypass)
                    e.memset(sp_, 0.0)
                    e.memset(sp_, 0.0)
                    e.tensor_tensor(msk[:, 0:4], mbb[:, 0:4], mbb[:, 4:8], ALU.min)
                    e.tensor_tensor(p2[:, 0:4], mbb[:, 0:4], mbb[:, 4:8], ALU.max)
                    e.tensor_tensor(msk[:, 4:6], msk[:, 0:2], msk[:, 2:4], ALU.min)
                    e.tensor_tensor(p2[:, 4:6], msk[:, 0:2], msk[:, 2:4], ALU.max)
                    e.memset(sp_, 0.0)
                    e.tensor_tensor(p2[:, 6:8], p2[:, 0:2], p2[:, 2:4], ALU.min)
                    e.memset(sp_, 0.0)
                    e.tensor_tensor(mbb[:, 0:2], p2[:, 4:6], p2[:, 6:8], ALU.min)
                    e.memset(sp_, 0.0)
                    e.tensor_tensor(msk[:, 1:2], msk[:, 4:5], msk[:, 5:6], ALU.max)
                    e.memset(sp_, 0.0)
                    e.tensor_tensor(mbb[:, 2:3], mbb[:, 0:1], mbb[:, 1:2], ALU.min)
                    e.memset(sp_, 0.0)
                    return e.tensor_tensor(rta[:, 2:3], msk[:, 1:2], mbb[:, 2:3], ALU.min)
                st(DVE, rtA, reads=["rt"], writes=["rt"])

                def rsg(e):
                    e.sign(p2[:, :], pr[:, :], bias=rta[:, 2:3])
                    return e.activation(pr[:, :], pr[:, :], AF.Exp)
                st(ACT, rsg, reads=["rt"], writes=["rt"])

                def rtB(e, t=t):
                    sp_ = rta[:, 7:8]
                    e.tensor_scalar(msk[:, :], p2[:, :], 1.0, 1.0, ALU.add, ALU.min)
                    e.tensor_tensor(mbb[:, 0:4], pr[:, 0:4], pr[:, 4:8], ALU.add)
                    e.memset(sp_, 0.0)
                    e.tensor_tensor(mbb[:, 4:6], mbb[:, 0:2], mbb[:, 2:4], ALU.add)
                    e.memset(sp_, 0.0)
                    e.tensor_tensor(rta[:, 3:4], mbb[:, 4:5], mbb[:, 5:6], ALU.add)
                    e.tensor_tensor(msk[:, :], msk[:, :], pr[:, :], ALU.mult)
                    e.reciprocal(rta[:, 4:5], rta[:, 3:4])
                    e.memset(sp_, 0.0)
                    e.memset(sp_, 0.0)
                    return e.tensor_scalar(csb[:, t * 8:(t + 1) * 8], msk[:, :], rta[:, 4:5], None,
                                           ALU.mult, ALU.bypass)
                st(DVE, rtB, reads=["rt"], writes=["rt", "c"])
                if DEBUG:
                    st(SP, (lambda e, t=t: e.dma_start(out=dbg_pr[:, t * 8:(t + 1) * 8], in_=pr[:, :])),
                       reads=["rt"], dma="d_out0")
                    if t == 5:
                        for i, sb in enumerate([p2, msk, mbb, rta]):
                            st(SP, (lambda e, i=i, sb=sb: e.dma_start(out=dbg_rt[:, i * 8:(i + 1) * 8], in_=sb[:, :])),
                               reads=["rt"], dma="d_out0")

        for t in range(NT):
            xslot = t % 2

            if t == 0:
                emit_xtr()
                emit_gkfront()

            if DEBUG and t == 0:
                for i, sb in enumerate([t4, gsp, eqv, ekv, ekdv, egtb]):
                    st(SP, (lambda e, i=i, sb=sb: e.dma_start(out=dbg_g[:, i * 4:(i + 1) * 4], in_=sb[:, :])),
                       reads=["gv"], dma="d_out0")
                st(SP, lambda e: e.dma_start(out=dbg_xb, in_=xb[:, :]), reads=["xb"], dma="d_out0")

            def q_mm(e):
                for c in range(8):
                    xc = xfm[:, c * 128:(c + 1) * 128]
                    e.matmul(PAp[:, 0:512], xc, W1[:, c * 1024:c * 1024 + 512], start=(c == 0), stop=(c == 7))
                    i = e.matmul(PAp[:, 512:1024], xc, W1[:, c * 1024 + 512:(c + 1) * 1024], start=(c == 0), stop=(c == 7))
                return i
            st(PE, q_mm, reads=["xfm", "W1a_lo", "W1a_hi"], writes=["PA0", "PA1"])

            def k_mm(e):
                for c in range(8):
                    xc = xfm[:, c * 128:(c + 1) * 128]
                    e.matmul(PBp[:, 0:512], xc, W1[:, 8192 + c * 1024:8192 + c * 1024 + 512], start=(c == 0), stop=(c == 7))
                    i = e.matmul(PBp[:, 512:1024], xc, W1[:, 8192 + c * 1024 + 512:8192 + (c + 1) * 1024], start=(c == 0), stop=(c == 7))
                return i
            st(PE, k_mm, reads=["xfm", "W1b_lo", "W1b_hi"], writes=["PB0", "PB1"])

            def q2_ev(e):
                for h in range(NHEAD):
                    i = e.tensor_scalar(q2[:, h * 256:(h + 1) * 256], PAp[:, h * 256:(h + 1) * 256],
                                        eqv[:, h:h + 1], 0.0625, ALU.mult, ALU.mult)
                return i
            st(DVE, q2_ev, reads=["PA0", "PA1", "gv"], writes=["q2"])

            def k1_ev(e):
                for h in range(NHEAD):
                    i = e.tensor_scalar(k1[:, h * 256:(h + 1) * 256], PBp[:, h * 256:(h + 1) * 256],
                                        ekv[:, h:h + 1], None, ALU.mult, ALU.bypass)
                return i
            st(DVE, k1_ev, reads=["PB0", "PB1", "gv"], writes=["k1"])

            def qtr(e):
                for c in range(8):
                    i = e.transpose(T1[:, c * 128:(c + 1) * 128], q2[:, c * 128:(c + 1) * 128], idbf[:, :])
                return i
            st(PE, qtr, reads=["q2", "idbf"], writes=["T1"])
            st(ACT, lambda e: e.activation(qfm[:, :], T1[:, :], AF.Copy), reads=["T1"], writes=["qfm"])

            def ktr(e):
                for c in range(8):
                    i = e.transpose(T0[:, c * 128:(c + 1) * 128], k1[:, c * 128:(c + 1) * 128], idbf[:, :])
                return i
            st(PE, ktr, reads=["k1", "idbf"], writes=["T0"])
            st(ACT, lambda e: e.activation(kfm[:, :], T0[:, :], AF.Copy), reads=["T0"], writes=["kfm"])

            def kd_ev(e):
                for h in range(NHEAD):
                    i = e.tensor_scalar(k1[:, h * 256:(h + 1) * 256], PBp[:, h * 256:(h + 1) * 256],
                                        ekdv[:, h:h + 1], None, ALU.mult, ALU.bypass)
                return i
            st(DVE, kd_ev, reads=["PB0", "PB1", "gv"], writes=["k1"])

            if t > 0:
                emit_seg2(t - 1)

            def v_mm(e, nt0, pb):
                for c in range(8):
                    xc = xfm[:, c * 128:(c + 1) * 128]
                    e.matmul(pb[:, 0:512], xc, W1[:, 16384 + c * 2048 + nt0 * 512:16384 + c * 2048 + (nt0 + 1) * 512],
                             start=(c == 0), stop=(c == 7))
                    i = e.matmul(pb[:, 512:1024], xc, W1[:, 16384 + c * 2048 + (nt0 + 1) * 512:16384 + c * 2048 + (nt0 + 2) * 512],
                                 start=(c == 0), stop=(c == 7))
                return i
            st(PE, (lambda e: v_mm(e, 0, PAp)), reads=["xfm", "W1c", "W1d"], writes=["PA0", "PA1"])
            st(DVE, lambda e: e.tensor_copy(vog[:, 0:1024], PAp[:, :]), reads=["PA0", "PA1"], writes=["vog"])

            def v_mm2(e):
                for c in range(8):
                    xc = xfm[:, c * 128:(c + 1) * 128]
                    e.matmul(PCp[:, :], xc, W1[:, 16384 + c * 2048 + 1024:16384 + c * 2048 + 1536],
                             start=(c == 0), stop=(c == 7))
                    i = e.matmul(PDp[:, :], xc, W1[:, 16384 + c * 2048 + 1536:16384 + c * 2048 + 2048],
                                 start=(c == 0), stop=(c == 7))
                return i
            st(PE, v_mm2, reads=["xfm", "W1c", "W1d"], writes=["PC", "PD"])

            def v_ev2(e):
                e.tensor_copy(vog[:, 1024:1536], PCp[:, :])
                return e.tensor_copy(vog[:, 1536:2048], PDp[:, :])
            st(DVE, v_ev2, reads=["PC", "PD"], writes=["vog"])

            def g_mm(e, nt0, pb):
                for c in range(8):
                    xc = xfm[:, c * 128:(c + 1) * 128]
                    e.matmul(pb[:, 0:512], xc, W2[:, c * 2048 + nt0 * 512:c * 2048 + (nt0 + 1) * 512],
                             start=(c == 0), stop=(c == 7))
                    i = e.matmul(pb[:, 512:1024], xc, W2[:, c * 2048 + (nt0 + 1) * 512:c * 2048 + (nt0 + 2) * 512],
                                 start=(c == 0), stop=(c == 7))
                return i
            st(PE, (lambda e: g_mm(e, 0, PBp)), reads=["xfm", "W2"], writes=["PB0", "PB1"])
            st(ACT, lambda e: e.activation(gsil[:, 0:1024], PBp[:, :], AF.Silu),
               reads=["PB0", "PB1"], writes=["gsil"])
            st(PE, (lambda e: g_mm(e, 2, PAp)), reads=["xfm", "W2"], writes=["PA0", "PA1"])
            st(ACT, lambda e: e.activation(gsil[:, 1024:2048], PAp[:, :], AF.Silu),
               reads=["PA0", "PA1"], writes=["gsil"])

            def at_pair(h):
                def at_mm(e, h=h):
                    e.matmul(PCp[:, 0:128], kfm[:, (2 * h) * 128:(2 * h + 1) * 128],
                             qfm[:, (2 * h) * 128:(2 * h + 1) * 128], start=True, stop=False)
                    return e.matmul(PCp[:, 0:128], kfm[:, (2 * h + 1) * 128:(2 * h + 2) * 128],
                                    qfm[:, (2 * h + 1) * 128:(2 * h + 2) * 128], start=False, stop=True)
                st(PE, at_mm, reads=["kfm", "qfm"], writes=["PC"])
                st(DVE, (lambda e: e.scalar_tensor_tensor(
                    At[:, 0:128], PCp[:, 0:128], 1.0, maskbf[:, :], ALU.mult, ALU.mult)),
                   reads=["PC", "maskbf"], writes=["At0"])

            at_pair(0)
            for h in range(NHEAD):
                OPp = PDp[:, :] if h % 2 == 0 else PBp[:, 0:512]
                ores = "PD" if h % 2 == 0 else "PB0"

                def o_mm(e, h=h, OPp=OPp):
                    e.matmul(OPp, At[:, 0:128], vog[:, h * 512:(h + 1) * 512],
                             start=True, stop=False)
                    e.matmul(OPp, qfm[:, (2 * h) * 128:(2 * h + 1) * 128],
                             S[:, (2 * h) * 512:(2 * h + 1) * 512], start=False, stop=False)
                    return e.matmul(OPp, qfm[:, (2 * h + 1) * 128:(2 * h + 2) * 128],
                                    S[:, (2 * h + 1) * 512:(2 * h + 2) * 512], start=False, stop=True)
                st(PE, o_mm, reads=["At0", "vog", "qfm", f"S{h}"], writes=[ores])

                if h + 1 < NHEAD:
                    at_pair(h + 1)

                def s_mm(e, h=h):
                    for cc in range(2):
                        bank = PAp[:, 0:512] if cc == 0 else PAp[:, 512:1024]
                        i = e.matmul(bank, k1[:, h * 256 + cc * 128:h * 256 + (cc + 1) * 128],
                                     vog[:, h * 512:(h + 1) * 512], start=True, stop=True)
                    return i
                st(PE, s_mm, reads=["k1", "vog"], writes=["PA0", "PA1"])

                def rms(e, h=h, OPp=OPp):
                    e.activation(q2[:, 0:512], OPp, AF.Square, accum_out=rhv[:, 0:1])
                    e.activation(rhv[:, 1:2], rhv[:, 0:1], AF.Ln, bias=epsv[:, 0:1], scale=1.0 / 512)
                    return e.activation(rhv[:, 2:3], rhv[:, 1:2], AF.Exp, scale=-0.5)
                st(ACT, rms, reads=[ores, "cstm"], writes=["rh", "q2"])

                st(DVE, (lambda e, h=h, OPp=OPp: e.tensor_scalar(
                    on_[:, h * 512:(h + 1) * 512], OPp, rhv[:, 2:3], None, ALU.mult, ALU.bypass)),
                   reads=[ores, "rh"], writes=["on_"])

                def s_ev(e, h=h):
                    e.scalar_tensor_tensor(S[:, (2 * h) * 512:(2 * h + 1) * 512],
                                           S[:, (2 * h) * 512:(2 * h + 1) * 512], egtb[:, h:h + 1],
                                           PAp[:, 0:512], ALU.mult, ALU.add)
                    return e.scalar_tensor_tensor(S[:, (2 * h + 1) * 512:(2 * h + 2) * 512],
                                                  S[:, (2 * h + 1) * 512:(2 * h + 2) * 512], egtb[:, h:h + 1],
                                                  PAp[:, 512:1024], ALU.mult, ALU.add)
                st(DVE, s_ev, reads=["PA0", "PA1", "gv"], writes=[f"S{h}"])

            st(DVE, lambda e: e.tensor_tensor(gsil[:, :], gsil[:, :], on_[:, :], ALU.mult),
               reads=["gsil", "on_"], writes=["gsil"])

            def ogtr(e, half):
                tb = Tb[half]
                for c in range(8):
                    i = e.transpose(tb[:, c * 128:(c + 1) * 128],
                                    gsil[:, half * 1024 + c * 128:half * 1024 + (c + 1) * 128], idbf[:, :])
                return i
            st(PE, (lambda e: ogtr(e, 0)), reads=["gsil", "idbf"], writes=["T0"])
            st(ACT, lambda e: e.activation(vog[:, 0:1024], T0[:, :], AF.Copy), reads=["T0"], writes=["vog"])
            st(PE, (lambda e: ogtr(e, 1)), reads=["gsil", "idbf"], writes=["T1"])
            st(ACT, lambda e: e.activation(vog[:, 1024:2048], T1[:, :], AF.Copy), reads=["T1"], writes=["vog"])

            if t + 1 < NT:
                cast_x(t + 1)
                emit_xtr()
                emit_gkfront()

            def wo_mm(e):
                for cc in range(16):
                    oc = vog[:, cc * 128:(cc + 1) * 128]
                    e.matmul(PCp[:, :], oc, W3[:, cc * 1024:cc * 1024 + 512], start=(cc == 0), stop=(cc == 15))
                    i = e.matmul(PDp[:, :], oc, W3[:, cc * 1024 + 512:(cc + 1) * 1024], start=(cc == 0), stop=(cc == 15))
                return i
            st(PE, wo_mm, reads=["vog", "W3"], writes=["PC", "PD"])

            def resid(e, xs=xslot):
                e.scalar_tensor_tensor(r_[:, 0:512], PCp[:, :], 1.0,
                                       xf[:, xs * 1024:xs * 1024 + 512], ALU.mult, ALU.add,
                                       accum_out=stv[:, 0:1])
                return e.scalar_tensor_tensor(r_[:, 512:1024], PDp[:, :], 1.0,
                                              xf[:, xs * 1024 + 512:(xs + 1) * 1024],
                                              ALU.mult, ALU.add, accum_out=stv[:, 1:2])
            st(DVE, resid, reads=["PC", "PD", f"xf{xslot}"], writes=["r_", "stt"])
            if t + 2 < NT:
                load_x(t + 2)

            def sqc(e):
                return e.activation(q2[:, :], r_[:, :], AF.Square, accum_out=stv[:, 8:9])
            st(ACT, sqc, reads=["r_", "stt"], writes=["stt", "q2"])

            def dvar(e):
                e.tensor_tensor(stv[:, 5:6], stv[:, 0:1], stv[:, 1:2], ALU.add)
                e.memset(stv[:, 11:12], 0.0)
                e.tensor_scalar(stv[:, 4:5], stv[:, 5:6], -1.0 / 1024, None, ALU.mult, ALU.bypass)
                e.memset(stv[:, 11:12], 0.0)
                e.tensor_tensor(stv[:, 6:7], stv[:, 4:5], stv[:, 4:5], ALU.mult)
                e.memset(stv[:, 11:12], 0.0)
                e.memset(stv[:, 11:12], 0.0)
                return e.scalar_tensor_tensor(stv[:, 7:8], stv[:, 8:9], 1.0 / 1024, stv[:, 6:7],
                                              ALU.mult, ALU.subtract)
            st(DVE, dvar, reads=["stt"], writes=["stt"])

            def rsc(e):
                e.activation(stv[:, 9:10], stv[:, 7:8], AF.Ln, bias=epsv[:, 0:1])
                return e.activation(stv[:, 10:11], stv[:, 9:10], AF.Exp, scale=-0.5)
            st(ACT, rsc, reads=["stt", "cstm"], writes=["stt"])

            st(DVE, lambda e: e.tensor_scalar(gsil[:, 0:1024], r_[:, :], stv[:, 4:5], stv[:, 10:11],
                                              ALU.add, ALU.mult),
               reads=["r_", "stt"], writes=["gsil"])

            if DEBUG and t == 0:
                st(SP, lambda e: e.dma_start(out=dbg_on0, in_=on_[:, :]), reads=["on_"], dma="d_out0")
                st(SP, lambda e: e.dma_start(out=dbg_r, in_=r_[:, :]), reads=["r_"], dma="d_out0")
                st(SP, lambda e: e.dma_start(out=dbg_stv, in_=stv[:, :]), reads=["stt"], dma="d_out0")

        emit_seg2(NT - 1)

        st(POOL, lambda e: e.dma_start(out=W2[0:1, 0:8192], in_=bin_h, max_dma_last_dim=8192),
           writes=["W2"], dma="d_bin")
        st(POOL, lambda e: e.dma_start(out=W2[0:1, 8192:16384], in_=bout_h, max_dma_last_dim=8192),
           writes=["W2"], dma="d_bin")

        SLOT = [(0, ["W1a_lo", "W1a_hi", "W1b_lo", "W1b_hi"]), (16384, ["W1c", "W1d"])]

        def load_expert(ee):
            base, res = SLOT[ee % 2]
            half = len(res) // 2
            st(POOL, (lambda e, ee=ee, base=base: e.dma_start(
                out=W1[:, base:base + 8192], in_=win_h[ee], max_dma_last_dim=8192)),
               writes=res[:half], dma=f"d_e{ee % 2}a")
            st(POOL, (lambda e, ee=ee, base=base: e.dma_start(
                out=W1[:, base + 8192:base + 16384], in_=wout_h[ee], max_dma_last_dim=8192)),
               writes=res[half:], dma=f"d_e{ee % 2}b")

        load_expert(0)
        load_expert(1)

        PAIRS = [(e_i, t) for e_i in range(NE) for t in range(NT)]

        def b_hmm(i):
            e_i, t = PAIRS[i]
            base, slot_res = SLOT[e_i % 2]
            HPp = PAp if i % 2 == 0 else PBp
            hres = ["PA0", "PA1"] if i % 2 == 0 else ["PB0", "PB1"]
            htm = q2 if i % 2 == 0 else xfm
            hmr = "q2" if i % 2 == 0 else "xfm"

            def h_mm(e, base=base, t=t, e_i=e_i, HPp=HPp):
                for c in range(8):
                    xc = xpfm[:, t * 1024 + c * 128:t * 1024 + (c + 1) * 128]
                    e.matmul(HPp[:, 0:512], xc, W1[:, base + c * 1024:base + c * 1024 + 512],
                             start=(c == 0), stop=False)
                    e.matmul(HPp[:, 512:1024], xc, W1[:, base + c * 1024 + 512:base + (c + 1) * 1024],
                             start=(c == 0), stop=False)
                e.matmul(HPp[:, 0:512], onesr[:, :], W2[0:1, e_i * 1024:e_i * 1024 + 512],
                         start=False, stop=True)
                return e.matmul(HPp[:, 512:1024], onesr[:, :], W2[0:1, e_i * 1024 + 512:(e_i + 1) * 1024],
                                start=False, stop=True)
            st(PE, h_mm, reads=["xpfm"] + slot_res[:len(slot_res) // 2] + ["W2", "onesr", "cstm"], writes=hres)

            def gelu(e, HPp=HPp, htm=htm):
                e.activation(htm[:, 0:512], HPp[:, 0:512], AF.Gelu_apprx_tanh)
                return e.activation(htm[:, 512:1024], HPp[:, 512:1024], AF.Gelu_apprx_tanh)
            st(ACT, gelu, reads=hres, writes=[hmr])

        b_hmm(0)
        b_hmm(1)
        for i, (e_i, t) in enumerate(PAIRS):
            base, slot_res = SLOT[e_i % 2]
            tp = i % 2

            htm = q2 if i % 2 == 0 else xfm
            hmr = "q2" if i % 2 == 0 else "xfm"

            def htr(e, tp=tp, htm=htm):
                tb = Tb[tp]
                for c in range(8):
                    i2 = e.transpose(tb[:, c * 128:(c + 1) * 128], htm[:, c * 128:(c + 1) * 128], idbf[:, :])
                return i2
            st(PE, htr, reads=[hmr, "idbf"], writes=[f"T{tp}"])
            st(ACT, (lambda e, tp=tp: e.activation(k1[:, :], Tb[tp][:, :], AF.Copy)),
               reads=[f"T{tp}"], writes=["k1"])

            if i + 2 < len(PAIRS):
                b_hmm(i + 2)

            if e_i == 0:
                st(DVE, (lambda e, t=t: e.memset(W3[:, t * 1024:(t + 1) * 1024], 0.0)),
                   writes=["W3"])

            def y_mm(e, base=base, e_i=e_i):
                for c in range(8):
                    hc = k1[:, c * 128:(c + 1) * 128]
                    e.matmul(PCp[:, :], hc, W1[:, base + 8192 + c * 1024:base + 8192 + c * 1024 + 512],
                             start=(c == 0), stop=False)
                    e.matmul(PDp[:, :], hc, W1[:, base + 8192 + c * 1024 + 512:base + 8192 + (c + 1) * 1024],
                             start=(c == 0), stop=False)
                e.matmul(PCp[:, :], onesr[:, :], W2[0:1, 8192 + e_i * 1024:8192 + e_i * 1024 + 512],
                         start=False, stop=True)
                return e.matmul(PDp[:, :], onesr[:, :], W2[0:1, 8192 + e_i * 1024 + 512:8192 + (e_i + 1) * 1024],
                                start=False, stop=True)
            st(PE, y_mm, reads=["k1"] + slot_res[len(slot_res) // 2:] + ["W2", "onesr", "cstm"], writes=["PC", "PD"])

            def y_acc(e, t=t, e_i=e_i):
                e.scalar_tensor_tensor(W3[:, t * 1024:t * 1024 + 512], PCp[:, :],
                                       csb[:, t * 8 + e_i:t * 8 + e_i + 1],
                                       W3[:, t * 1024:t * 1024 + 512], ALU.mult, ALU.add)
                return e.scalar_tensor_tensor(W3[:, t * 1024 + 512:(t + 1) * 1024], PDp[:, :],
                                              csb[:, t * 8 + e_i:t * 8 + e_i + 1],
                                              W3[:, t * 1024 + 512:(t + 1) * 1024], ALU.mult, ALU.add)
            st(DVE, y_acc, reads=["PC", "PD", "c", "W3"], writes=["W3"])

            if t == NT - 1 and e_i + 2 < NE:
                load_expert(e_i + 2)

        def c_tr(t):
            tp = t % 2
            xtm = q2 if tp == 0 else k1
            xres = "q2" if tp == 0 else "k1"

            def ctr(e, t=t, tp=tp):
                for c in range(8):
                    i = e.transpose(Tb[tp][:, c * 128:(c + 1) * 128],
                                    xpfm[:, t * 1024 + c * 128:t * 1024 + (c + 1) * 128], idbf[:, :])
                return i
            st(PE, ctr, reads=["xpfm", "idbf"], writes=[f"T{tp}"])
            st(ACT, (lambda e, tp=tp, xtm=xtm: e.activation(xtm[:, :], Tb[tp][:, :], AF.Copy)),
               reads=[f"T{tp}"], writes=[xres])

        c_tr(0)
        c_tr(1)
        for t in range(NT):
            tp = t % 2
            xtm = q2 if tp == 0 else k1
            xres = "q2" if tp == 0 else "k1"

            def cr(e, t=t, xtm=xtm):
                return e.scalar_tensor_tensor(r_[:, :], W3[:, t * 1024:(t + 1) * 1024], 1.0, xtm[:, :],
                                              ALU.mult, ALU.add, accum_out=stv[:, 2:3])
            st(DVE, cr, reads=["W3", xres], writes=["r_", "stt"])

            if t + 2 < NT:
                c_tr(t + 2)

            def csqc(e):
                e.activation(kfm[:, :], r_[:, :], AF.Square, accum_out=stv[:, 8:9])
                return e.activation(stv[:, 6:7], stv[:, 2:3], AF.Square, scale=1.0 / 1024)
            st(ACT, csqc, reads=["r_", "stt"], writes=["stt", "kfm"])

            def cdvar(e):
                e.tensor_scalar(stv[:, 3:4], stv[:, 2:3], -1.0 / 1024, None, ALU.mult, ALU.bypass)
                return e.scalar_tensor_tensor(stv[:, 7:8], stv[:, 8:9], 1.0 / 1024, stv[:, 6:7],
                                              ALU.mult, ALU.subtract)
            st(DVE, cdvar, reads=["stt"], writes=["stt"])

            def crsc(e):
                e.activation(stv[:, 9:10], stv[:, 7:8], AF.Ln, bias=epsv[:, 0:1])
                return e.activation(stv[:, 10:11], stv[:, 9:10], AF.Exp, scale=-0.5)
            st(ACT, crsc, reads=["stt", "cstm"], writes=["stt"])

            xs = t % 2
            st(DVE, (lambda e, xs=xs: e.tensor_scalar(xf[:, xs * 1024:(xs + 1) * 1024], r_[:, :],
                                                      stv[:, 3:4], stv[:, 10:11], ALU.add, ALU.mult)),
               reads=["r_", "stt"], writes=[f"xf{xs}"])
            st(SP, (lambda e, t=t, xs=xs: e.dma_start(
                out=out_h[t * 128:(t + 1) * 128, :], in_=xf[:, xs * 1024:(xs + 1) * 1024])),
               reads=[f"xf{xs}"], dma=f"d_out{xs}")

        if DEBUG:
            dxp = nc.dram_tensor("dbg_xp", [128, 16384], BF16, kind="ExternalOutput").ap()
            dc = nc.dram_tensor("dbg_c", [128, 128], F32, kind="ExternalOutput").ap()
            dy = nc.dram_tensor("dbg_y", [128, 16384], BF16, kind="ExternalOutput").ap()
            don = nc.dram_tensor("dbg_on", [128, 2048], BF16, kind="ExternalOutput").ap()
            st(SP, lambda e: e.dma_start(out=dxp, in_=xpfm[:, :]), reads=["xpfm"], dma="d_out0")
            st(SP, lambda e: e.dma_start(out=dc, in_=csb[:, :]), reads=["c"], dma="d_out0")
            st(SP, lambda e: e.dma_start(out=dy, in_=W3[:, :]), reads=["W3"], dma="d_out0")
            st(SP, lambda e: e.dma_start(out=don, in_=on_[:, :]), reads=["on_"], dma="d_out0")

        block = E(nc.Block())
        eng_map = {}

        def runner(engname):
            def run(eng):
                for (en, waits, emit, comp) in plan.stages:
                    if en != engname:
                        continue
                    for semname, val in waits:
                        eng.wait_ge(sems[semname], val)
                    inst = emit(eng)
                    inst.then_inc(sems[comp[0]], 16 if comp[0].startswith("d_") else 1)
                if engname == SP:
                    eng.wait_ge(sems["d_out0"], plan.semcnt.get("d_out0", 0))
                    eng.wait_ge(sems["d_out1"], plan.semcnt.get("d_out1", 0))
            return run

        block.tensor(runner(PE))
        block.vector(runner(DVE))
        block.scalar(runner(ACT))
        block.sync(runner(SP))
        block.gpsimd(runner(POOL))

    return nc


def _lay8(w):
    n = w.shape[1]
    return np.ascontiguousarray(w.reshape(8, 128, n).transpose(1, 0, 2).reshape(128, 8 * n))


def _lay16(w):
    n = w.shape[1]
    return np.ascontiguousarray(w.reshape(16, 128, n).transpose(1, 0, 2).reshape(128, 16 * n))


def kernel(L0_attn_x, L0_attn_Wq, L0_attn_Wk, L0_attn_Wv, L0_attn_Wg, L0_attn_Wgk,
           L0_attn_bgk, L0_attn_Wo, L0_attn_gnorm_w, L0_ffn_router_w, L0_ffn_w_in,
           L0_ffn_b_in, L0_ffn_w_out, L0_ffn_b_out):
    x = np.asarray(L0_attn_x, dtype=np.float32)
    shared = {
        "wq": _lay8(np.asarray(L0_attn_Wq, np.float32)),
        "wk": _lay8(np.asarray(L0_attn_Wk, np.float32)),
        "wv": _lay8(np.asarray(L0_attn_Wv, np.float32)),
        "wg": _lay8(np.asarray(L0_attn_Wg, np.float32)),
        "wo": _lay16(np.asarray(L0_attn_Wo, np.float32)),
        "wgk": _lay8(np.asarray(L0_attn_Wgk, np.float32)),
        "rw": _lay8(np.asarray(L0_ffn_router_w, np.float32)),
        "bgk": np.ascontiguousarray(np.asarray(L0_attn_bgk, np.float32).reshape(1, 4)),
        "bin": np.ascontiguousarray(np.asarray(L0_ffn_b_in, np.float32).reshape(1, 8192)),
        "bout": np.ascontiguousarray(np.asarray(L0_ffn_b_out, np.float32).reshape(1, 8192)),
        "gnw4": np.ascontiguousarray(np.asarray(L0_attn_gnorm_w, np.float32).reshape(4, 128).T),
        "idbf": np.eye(128).astype(ml_dtypes.bfloat16),
        "utri": np.ascontiguousarray(np.triu(np.ones((128, 128), np.float32)) * (-1.0 / 16.0)),
        "onesn": np.full((128, 128), -1.0 / 16.0, np.float32),
        "maskbf": np.triu(np.ones((128, 128))).astype(ml_dtypes.bfloat16),
    }
    w_in = np.asarray(L0_ffn_w_in, np.float32)
    w_out = np.asarray(L0_ffn_w_out, np.float32)
    for e in range(NE):
        shared[f"win{e}"] = _lay8(w_in[e])
        shared[f"wout{e}"] = _lay8(w_out[e])

    nc = build_nc()
    in_maps = []
    for c in range(8):
        m = dict(shared)
        m["x"] = np.ascontiguousarray(x[c])
        in_maps.append(m)
    res = run_bass_kernel_spmd(nc, in_maps, core_ids=list(range(8)))
    out = np.stack([np.asarray(res.results[c]["out"]) for c in range(8)], axis=0)
    return out.astype(np.float32)
